# version 14
# baseline (speedup 1.0000x reference)
import numpy as np
import ml_dtypes
import concourse.bass as bass
import concourse.mybir as mybir
from concourse.bass_utils import run_bass_kernel_spmd
from contextlib import ExitStack

F32 = mybir.dt.float32
BF16 = mybir.dt.bfloat16
AX = mybir.AxisListType.X
AF = mybir.ActivationFunctionType
OP = mybir.AluOpType

C23 = 12582912.0
C52 = 6755399441055744.0
NEG = -1.0e30

E = 2048; HD = 2048; KVD = 512; D = 64; H = 32; NKV = 8; GQ = 4; II = 8192
ECN = 16; ICN = 64; HDCN = 16

N_PHASES = 6


def _host_consts(T):
    import jax
    import jax.numpy as jnp
    with jax.default_device(jax.devices("cpu")[0]):
        inv_freq = 1.0 / 10000.0 ** (jnp.arange(0, D, 2, dtype=jnp.float32) / D)
        t_ = jnp.arange(T, dtype=jnp.float32)
        freqs = t_[:, None] * inv_freq[None, :]
        emb = jnp.concatenate([freqs, freqs], axis=-1)
        cos64 = np.asarray(jnp.cos(emb), np.float32).T
        sin64 = np.asarray(jnp.sin(emb), np.float32).T
    cos_fm = np.ascontiguousarray(np.concatenate([cos64, cos64], 0).astype(np.float32))
    sin_fm = np.ascontiguousarray(np.concatenate([sin64, sin64], 0).astype(np.float32))
    idb = np.eye(128, dtype=ml_dtypes.bfloat16)
    idf = np.eye(128, dtype=np.float32)
    R = np.zeros((64, 64), np.float32)
    R[np.arange(32), np.arange(32) + 32] = -1.0
    R[np.arange(32) + 32, np.arange(32)] = 1.0
    RT = np.zeros((128, 128), np.float32)
    RT[:64, :64] = R.T
    RT[64:, 64:] = R.T
    ones1 = np.ones((1, 128), np.float32)
    iq = np.arange(128)[:, None]
    jk = np.arange(128)[None, :]
    m_old = np.where(jk > iq, 0.0, NEG).astype(np.float32)
    m_cur = np.where(jk <= iq, 0.0, NEG).astype(np.float32)
    full = np.zeros((128, 128), np.float32)
    neg = np.full((128, 128), NEG, np.float32)
    mask2 = np.ascontiguousarray(np.concatenate([m_old, full, m_cur], 1).astype(ml_dtypes.bfloat16))
    mask1 = np.ascontiguousarray(np.concatenate([neg, full, m_cur], 1).astype(ml_dtypes.bfloat16))
    mask0 = np.ascontiguousarray(np.concatenate([neg, neg, m_cur], 1).astype(ml_dtypes.bfloat16))
    return dict(cos_fm=cos_fm, sin_fm=sin_fm, idb=np.asarray(idb), idf=idf, RT=RT,
                ones1=ones1, mask0=mask0, mask1=mask1, mask2=mask2)


class Ctx:
    pass


def build_nc(T):
    c = Ctx()
    c.T = T
    c.TCN = T // 128
    c.SCN = T // 512
    nc = bass.Bass("TRN2", target_bir_lowering=False)
    c.nc = nc

    def din(name, shape, dt=F32):
        return nc.dram_tensor(name, shape, dt, kind="ExternalInput").ap()

    def scrd(name, shape, dt):
        return nc.dram_tensor(name, shape, dt, kind="Internal").ap()

    c.x_d = din("x", [T, E])
    c.Wq_d = din("Wq", [E, HD]); c.Wk_d = din("Wk", [E, KVD]); c.Wv_d = din("Wv", [E, KVD])
    c.Wo_d = din("Wo", [HD, E])
    c.bq_d = din("bq_r", [128, HDCN]); c.bk_d = din("bk_r", [128, 4]); c.bv_d = din("bv_r", [128, 4])
    c.bo_d = din("bo_row", [1, E])
    c.sinks_d = din("sinks_b", [128, H])
    c.wgT_d = din("wgT", [E, II]); c.wuT_d = din("wuT", [E, II]); c.wdT_d = din("wdT", [II, E])
    c.cos_d = din("cos_fm", [128, T]); c.sin_d = din("sin_fm", [128, T])
    c.idb_d = din("idb", [128, 128], BF16); c.idf_d = din("idf", [128, 128])
    c.RT_d = din("RT", [128, 128]); c.on1_d = din("ones1", [1, 128])
    c.mask_d = [din("mask0", [128, 384], BF16), din("mask1", [128, 384], BF16),
                din("mask2", [128, 384], BF16)]
    c.y_d = nc.dram_tensor("y", [T, E], F32, kind="ExternalOutput").ap()

    c.wqh_d = scrd("wq_hi", [E, HD], BF16); c.wql_d = scrd("wq_lo", [E, HD], BF16)
    c.wkh_d = scrd("wk_hi", [E, KVD], BF16); c.wkl_d = scrd("wk_lo", [E, KVD], BF16)
    c.wvh_d = scrd("wv_hi", [E, KVD], BF16); c.wvl_d = scrd("wv_lo", [E, KVD], BF16)
    c.woh_d = scrd("wo_hi", [HD, E], BF16); c.wol_d = scrd("wo_lo", [HD, E], BF16)
    c.xnh_d = scrd("xnh_sp", [ECN, 128, T], BF16)
    c.xnl_d = scrd("xnl_sp", [ECN, 128, T], BF16)
    c.xq_d = scrd("xq_sp", [ECN, 128, T], BF16)
    c.oh_d = scrd("oh_sp", [HDCN, 128, T], BF16)
    c.ol_d = scrd("ol_sp", [HDCN, 128, T], BF16)
    c.ysp_d = scrd("y_sp", [T, E], F32)
    c.z_d = scrd("z_sp", [T, II], F32)
    c.zq_d = scrd("zq_sp", [ICN, 128, T], BF16)

    c.ges = ExitStack()

    def gsb(name, shape, dt):
        return c.ges.enter_context(nc.sbuf_tensor(name, shape, dt))

    c.gsb = gsb
    c.sem = lambda name: c.ges.enter_context(nc.semaphore(name))

    class DQ:
        def __init__(self, name, nslots):
            self.sems = [c.sem(f"{name}{i}") for i in range(nslots)]
            self.tot = [0] * nslots
        def inc(self, inst, slot=0):
            inst.then_inc(self.sems[slot], 16)
            self.tot[slot] += 16
            return (slot, self.tot[slot])
        def wait(self, eng, mark):
            eng.wait_ge(self.sems[mark[0]], mark[1])
    c.DQ = DQ

    c.u_t = gsb("u_t", [128, 1], F32)
    c.cr_t = gsb("cr_t", [128, 1], F32)
    c.r127 = gsb("r127", [128, 1], F32)
    c.xs_all = gsb("xs_all", [128, c.TCN], F32)
    c.xswg = gsb("xswg", [128, c.TCN], F32)
    c.xswu = gsb("xswu", [128, c.TCN], F32)
    c.s2w_all = gsb("s2w_all", [128, c.TCN], F32)
    c.s2i_all = gsb("s2i_all", [128, c.TCN], F32)
    c.rmax_z = gsb("rmax_z", [128, c.TCN], F32)
    c.wsum = gsb("wsum", [128, 192], F32)
    c.wsc = gsb("wsc", [128, 8], F32)
    c.idb_t = gsb("idb_t", [128, 128], BF16)
    c.idf_t = gsb("idf_t", [128, 128], F32)
    c.RT_t = gsb("RT_t", [128, 128], F32)
    c.on1_t = gsb("on1_t", [1, 128], F32)
    c.nsink = gsb("nsink", [128, H], F32)
    c.bq_t = gsb("bq_t", [128, HDCN], F32)
    c.bk_t = gsb("bk_t", [128, 4], F32)
    c.bv_t = gsb("bv_t", [128, 4], F32)
    c.dum = gsb("dum", [128, 8], F32)
    c.eps_t = gsb("eps_t", [128, 1], F32)
    c.c125 = gsb("c125", [128, 1], F32)

    c.phz = c.sem("phz")
    c.PH = 0

    c.blk = c.ges.enter_context(nc.Block())
    phases = [(_phaseW, False), (_phase0, True), (_phase1, True),
              (_phase15, True), (_phase2, True), (_phase3, True)]
    for ph, bump in phases[:N_PHASES]:
        ph(c)
        if bump:
            c.PH += 1
    c.ges.close()
    return nc


def _phaseW(c):
    nc, blk = c.nc, c.blk
    es = ExitStack()
    sb = lambda n, s, d: es.enter_context(nc.sbuf_tensor(n, s, d))
    psm = lambda n, s, d=F32: es.enter_context(nc.psum_tensor(n, s, d))

    WJ = []
    for (srcd, hid, lod, rows, cols) in [
        (c.Wq_d, c.wqh_d, c.wql_d, E, HD), (c.Wk_d, c.wkh_d, c.wkl_d, E, KVD),
        (c.Wv_d, c.wvh_d, c.wvl_d, E, KVD), (c.Wo_d, c.woh_d, c.wol_d, HD, E),
    ]:
        for r0 in range(0, rows, 128):
            WJ.append((srcd, hid, lod, r0, cols))

    wbuf = [sb("wbuf0", [128, 2048], F32), sb("wbuf1", [128, 2048], F32)]
    whib = [sb("whib0", [128, 2048], BF16), sb("whib1", [128, 2048], BF16)]
    wlob = [sb("wlob0", [128, 2048], BF16), sb("wlob1", [128, 2048], BF16)]

    qC = c.DQ("qC", 1)
    qWl = c.DQ("qWl", 2)
    qWs = c.DQ("qWs", 2)
    vWs = c.sem("vWs")
    mk = {}

    @blk.scalar
    def _(eng):
        for j, (srcd, hid, lod, r0, cols) in enumerate(WJ):
            if j >= 2:
                eng.wait_ge(vWs, j - 1)
            mk[("l", j)] = qWl.inc(eng.dma_start(out=wbuf[j % 2][:, 0:cols],
                                                 in_=srcd[r0:r0 + 128, :]), j % 2)
        pass

    @blk.sync
    def _(eng):
        for dst, src in [(c.idb_t, c.idb_d), (c.idf_t, c.idf_d), (c.RT_t, c.RT_d),
                         (c.on1_t, c.on1_d), (c.nsink, c.sinks_d), (c.bq_t, c.bq_d),
                         (c.bk_t, c.bk_d), (c.bv_t, c.bv_d)]:
            mk["consts"] = qC.inc(eng.dma_start(out=dst[:], in_=src), 0)
        for j, (srcd, hid, lod, r0, cols) in enumerate(WJ):
            eng.wait_ge(vWs, j + 1)
            qWs.inc(eng.dma_start(out=hid[r0:r0 + 128, :],
                                  in_=whib[j % 2][:, 0:cols]), j % 2)
            mk[("st", j)] = qWs.inc(eng.dma_start(out=lod[r0:r0 + 128, :],
                                                  in_=wlob[j % 2][:, 0:cols]), j % 2)

    @blk.vector
    def _(eng):
        eng.memset(c.u_t[:], 0.75)
        eng.memset(c.r127[:], 127.0)
        eng.memset(c.rmax_z[:], 0.0)
        eng.memset(c.eps_t[:], 1e-5)
        eng.memset(c.c125[:], 0.125)
        eng.drain()
        eng.tensor_scalar(out=c.u_t[:], in0=c.u_t[:], scalar1=C52, scalar2=C52,
                          op0=OP.add, op1=OP.subtract)
        eng.drain()
        eng.tensor_scalar(out=c.cr_t[:], in0=c.u_t[:], scalar1=C52, scalar2=C23,
                          op0=OP.mult, op1=OP.add)
        eng.reciprocal(out=c.r127[:], in_=c.r127[:])
        qC.wait(eng, mk["consts"])
        eng.tensor_scalar(out=c.nsink[:], in0=c.nsink[:], scalar1=-1.0,
                          scalar2=None, op0=OP.mult)
        for j, (srcd, hid, lod, r0, cols) in enumerate(WJ):
            qWl.wait(eng, mk[("l", j)])
            if j >= 2:
                qWs.wait(eng, mk[("st", j - 2)])
            eng.tensor_copy(out=whib[j % 2][:, 0:cols], in_=wbuf[j % 2][:, 0:cols])
            eng.drain()
            eng.tensor_sub(out=wlob[j % 2][:, 0:cols], in0=wbuf[j % 2][:, 0:cols],
                           in1=whib[j % 2][:, 0:cols]).then_inc(vWs, 1)
        qWs.wait(eng, mk[("st", len(WJ) - 2)])
        qWs.wait(eng, mk[("st", len(WJ) - 1)])

    c.qC = qC
    c.mkC = mk["consts"]
    c._w_es = es


def _phase0(c):
    nc, blk, T, TCN = c.nc, c.blk, c.T, c.TCN
    es = ExitStack()
    sb = lambda n, s, d: es.enter_context(nc.sbuf_tensor(n, s, d))
    psm = lambda n, s, d=F32: es.enter_context(nc.psum_tensor(n, s, d))

    xb = [sb("xb0", [128, E], F32), sb("xb1", [128, E], F32)]
    xnt = sb("xnt", [128, E], F32)
    vqt = sb("vqt", [128, E], F32)
    m23t = sb("m23t", [128, E], F32)
    ddt = sb("ddt", [128, E], F32)
    xqt = sb("xqt", [128, E], BF16)
    xnhb = sb("xnhb", [128, E], BF16)
    xnlb = sb("xnlb", [128, E], BF16)
    sq_scr = sb("sq_scr", [128, E], F32)
    stt = sb("stt", [128, 8], F32)
    stg = [sb(f"stg{i}", [128, E], BF16) for i in range(6)]
    p0ps = [psm("p0ps0", [128, 1024], BF16), psm("p0ps1", [128, 1024], BF16)]

    qX = c.DQ("qX", 2); qP = c.DQ("qP", 2)
    v0 = c.sem("v0"); s0 = c.sem("s0"); g0 = c.sem("g0"); t0 = c.sem("t0")
    mk = {}
    NG = 6
    PB = 4 * c.PH

    def v0_var(tc): return 11 * tc + 1
    def v0_xdone(tc): return 11 * tc + 2
    def v0_v(tc): return 11 * tc + 3
    def v0_split(tc): return 11 * tc + 4
    def v0_xq(tc): return 11 * tc + 5
    def v0_ev(tc, g): return 11 * tc + 6 + g
    def v0_stage(tc): return 11 * (tc + 1)

    @blk.sync
    def _(eng):
        eng.wait_ge(c.phz, PB)
        for tc in range(TCN + 1):
            if tc < TCN:
                if tc >= 2:
                    eng.wait_ge(v0, v0_xdone(tc - 2))
                    eng.wait_ge(s0, 2 * (tc - 2) + 1)
                mk[("x", tc)] = qX.inc(eng.dma_start(
                    out=xb[tc % 2][:],
                    in_=c.x_d[tc * 128:(tc + 1) * 128, :]), tc % 2)
            tcs = tc - 1
            if 0 <= tcs < TCN:
                eng.wait_ge(v0, v0_stage(tcs))
                pb = (tcs % 2) * 3
                for i, dd_ in enumerate([c.xnh_d, c.xnl_d, c.xq_d]):
                    sp3 = stg[pb + i][:].rearrange("p (ec t) -> p ec t", ec=ECN)
                    mk[("sp", tcs)] = qP.inc(eng.dma_start(
                        out=dd_.transpose([1, 0, 2])[:, :, tcs * 128:(tcs + 1) * 128],
                        in_=sp3), tcs % 2)

    @blk.scalar
    def _(eng):
        eng.wait_ge(c.phz, PB)
        for tc in range(TCN):
            qX.wait(eng, mk[("x", tc)])
            eng.activation(out=sq_scr[:], in_=xb[tc % 2][:], func=AF.Square,
                           bias=0.0, scale=1.0,
                           accum_out=stt[:, 2:3]).then_inc(s0, 1)
            eng.wait_ge(v0, v0_var(tc))
            eng.activation(out=stt[:, 4:5], in_=stt[:, 3:4], func=AF.Sqrt,
                           bias=c.eps_t[:], scale=1.0).then_inc(s0, 1)
        eng.activation(out=c.dum[:, 0:1], in_=c.u_t[:], func=AF.Copy, bias=0.0,
                       scale=0.0).then_inc(c.phz, 1)

    @blk.gpsimd
    def _(eng):
        eng.wait_ge(c.phz, PB)
        eng.memset(c.dum[:, 2:3], 0.0).then_inc(c.phz, 1)

    @blk.vector
    def _(eng):
        eng.wait_ge(c.phz, PB)
        for tc in range(TCN):
            qX.wait(eng, mk[("x", tc)])
            if tc >= 1:
                eng.wait_ge(t0, NG * tc)
            x_t = xb[tc % 2]
            eng.tensor_reduce(out=stt[:, 0:1], in_=x_t[:], axis=AX, op=OP.add)
            eng.drain()
            eng.tensor_scalar(out=stt[:, 1:2], in0=stt[:, 0:1],
                              scalar1=1.0 / 2048.0, scalar2=None, op0=OP.mult)
            eng.wait_ge(s0, 2 * tc + 1)
            eng.drain()
            eng.tensor_scalar(out=stt[:, 3:4], in0=stt[:, 2:3],
                              scalar1=1.0 / 2048.0, scalar2=None, op0=OP.mult)
            eng.tensor_mul(out=stt[:, 5:6], in0=stt[:, 1:2], in1=stt[:, 1:2])
            eng.drain()
            eng.tensor_sub(out=stt[:, 3:4], in0=stt[:, 3:4],
                           in1=stt[:, 5:6]).then_inc(v0, 1)
            eng.wait_ge(s0, 2 * tc + 2)
            eng.reciprocal(out=stt[:, 5:6], in_=stt[:, 4:5])
            eng.drain()
            eng.tensor_scalar(out=xnt[:], in0=x_t[:], scalar1=stt[:, 1:2],
                              scalar2=stt[:, 5:6], op0=OP.subtract, op1=OP.mult)
            eng.drain()
            eng.tensor_copy(out=xnhb[:], in_=xnt[:]).then_inc(v0, 1)
            eng.tensor_reduce(out=stt[:, 6:7], in_=xnt[:], axis=AX, op=OP.max,
                              apply_absolute_value=True)
            eng.drain()
            eng.tensor_scalar(out=stt[:, 6:7], in0=stt[:, 6:7], scalar1=c.r127[:],
                              scalar2=1e-5, op0=OP.mult, op1=OP.add)
            eng.drain()
            eng.tensor_copy(out=c.xs_all[:, tc:tc + 1], in_=stt[:, 6:7])
            eng.reciprocal(out=stt[:, 7:8], in_=stt[:, 6:7])
            eng.drain()
            eng.tensor_scalar(out=vqt[:], in0=xnt[:], scalar1=stt[:, 7:8],
                              scalar2=None, op0=OP.mult)
            eng.drain()
            eng.tensor_scalar(out=m23t[:], in0=vqt[:], scalar1=C23, scalar2=C23,
                              op0=OP.add, op1=OP.subtract)
            eng.tensor_scalar(out=ddt[:], in0=vqt[:], scalar1=C52, scalar2=C52,
                              op0=OP.add, op1=OP.subtract).then_inc(v0, 1)
            eng.drain()
            eng.tensor_sub(out=ddt[:], in0=ddt[:], in1=m23t[:])
            eng.tensor_sub(out=xnlb[:], in0=xnt[:], in1=xnhb[:]).then_inc(v0, 1)
            eng.drain()
            eng.scalar_tensor_tensor(out=xqt[:], in0=ddt[:], scalar=c.u_t[:],
                                     in1=m23t[:], op0=OP.mult,
                                     op1=OP.add).then_inc(v0, 1)
            if tc >= 2:
                qP.wait(eng, mk[("sp", tc - 2)])
            pb = (tc % 2) * 3
            for g in range(NG):
                eng.wait_ge(t0, NG * tc + g + 1)
                eng.tensor_copy(
                    out=stg[pb + g // 2][:, (g % 2) * 1024:(g % 2) * 1024 + 1024],
                    in_=p0ps[g % 2][:]).then_inc(v0, 1)
        qP.wait(eng, mk[("sp", TCN - 2)])
        qP.wait(eng, mk[("sp", TCN - 1)])
        qX.wait(eng, mk[("x", TCN - 2)])
        qX.wait(eng, mk[("x", TCN - 1)])
        eng.memset(c.dum[:, 1:2], 0.0).then_inc(c.phz, 1)

    @blk.tensor
    def _(eng):
        eng.wait_ge(c.phz, PB)
        c.qC.wait(eng, c.mkC)
        for tc in range(TCN):
            srcs = [xnhb, xnhb, xnlb, xnlb, xqt, xqt]
            for g in range(NG):
                eng.wait_ge(v0, v0_split(tc) if g < 4 else v0_xq(tc))
                gi = tc * NG + g
                if gi >= 2:
                    eng.wait_ge(v0, v0_ev((gi - 2) // NG, (gi - 2) % NG))
                cb = (g % 2) * 1024
                for k in range(8):
                    i = eng.transpose(p0ps[g % 2][:, k * 128:(k + 1) * 128],
                                      srcs[g][:, cb + k * 128:cb + (k + 1) * 128],
                                      c.idb_t[:])
                    if k == 7:
                        i.then_inc(t0, 1)
        eng.wait_ge(v0, v0_ev(TCN - 1, NG - 1))
        eng.transpose(p0ps[0][:, 0:128], xqt[:, 0:128],
                      c.idb_t[:]).then_inc(c.phz, 1)

    es.close()
    c._w_es.close()


def _phase1(c):
    nc, blk, T, TCN, SCN = c.nc, c.blk, c.T, c.TCN, c.SCN
    es = ExitStack()
    sb = lambda n, s, d: es.enter_context(nc.sbuf_tensor(n, s, d))
    psm = lambda n, s, d=F32: es.enter_context(nc.psum_tensor(n, s, d))
    PB = 4 * c.PH

    cos_t = sb("cos_t", [128, T], F32)
    sin_t = sb("sin_t", [128, T], F32)
    mask_t = sb("mask_t", [128, 3 * 384], BF16)
    xnh_s = sb("xnh_s", [128, 16 * 512], BF16)
    xnl_s = sb("xnl_s", [128, 16 * 512], BF16)
    wp_h = [sb("wp_h0", [128, 768], BF16), sb("wp_h1", [128, 768], BF16)]
    wp_l = [sb("wp_l0", [128, 768], BF16), sb("wp_l1", [128, 768], BF16)]
    qsb = [sb("qsb0", [128, 512], F32), sb("qsb1", [128, 512], F32)]
    rt1 = sb("rt1", [128, 512], F32)
    rt2 = sb("rt2", [128, 512], F32)
    qr_h = sb("qr_h", [128, 16 * 512], BF16)
    qr_l = sb("qr_l", [128, 16 * 512], BF16)
    kr_h = sb("kr_h", [128, 4 * 768], BF16)
    kr_l = sb("kr_l", [128, 4 * 768], BF16)
    vr_h = sb("vr_h", [128, 6 * 512], BF16)
    vr_l = sb("vr_l", [128, 6 * 512], BF16)
    p_sb2 = [sb("p_sb0", [128, 4 * 384], F32), sb("p_sb1", [128, 4 * 384], F32)]
    smt2 = [sb("smt0", [128, 24], F32), sb("smt1", [128, 24], F32)]
    pt_h2 = [sb("pt_h0", [128, 3 * 512], BF16), sb("pt_h1", [128, 3 * 512], BF16)]
    pt_l2 = [sb("pt_l0", [128, 3 * 512], BF16), sb("pt_l1", [128, 3 * 512], BF16)]
    ost_h = [sb("ost_h0", [128, 2048], BF16), sb("ost_h1", [128, 2048], BF16)]
    ost_l = [sb("ost_l0", [128, 2048], BF16), sb("ost_l1", [128, 2048], BF16)]

    p1q = [psm("p1qA", [128, 2048], F32), psm("p1qB", [128, 2048], F32)]

    def acc_ap(oi):
        if oi < 4:
            return p1q[0][:, oi * 512:(oi + 1) * 512]
        return p1q[1][:, (oi - 4) * 512:(oi - 3) * 512]

    q1n = c.DQ("q1n", 1)
    q1m = c.DQ("q1m", 1)
    q1w = c.DQ("q1w", 2)
    q1o = c.DQ("q1o", 2)
    t1a = c.sem("t1a"); t1mm = c.sem("t1mm"); t1r = c.sem("t1r")
    s1e = c.sem("s1e"); v1r = c.sem("v1r")
    t1s = c.sem("t1s"); v1m = c.sem("v1m"); s1x = c.sem("s1x")
    v1d = c.sem("v1d"); t1p = c.sem("t1p")
    s1h = c.sem("s1h"); v1l = c.sem("v1l"); t1v = c.sem("t1v")
    s1o = c.sem("s1o"); v1o = c.sem("v1o")
    s1n = c.sem("s1n")
    mk = {}

    OCS = ([("q", i) for i in range(16)] + [("k", i) for i in range(4)]
           + [("v", i) for i in range(4)])
    GRP = [OCS[0:6], OCS[6:12], OCS[12:18], OCS[18:24]]

    def srcs_of(kind):
        return {"q": (c.wqh_d, c.wql_d), "k": (c.wkh_d, c.wkl_d),
                "v": (c.wvh_d, c.wvl_d)}[kind]

    def grp_runs2(gi):
        out = []
        off = 0
        k = 0
        members = GRP[gi]
        while k < len(members):
            kind, i = members[k]
            hd_, ld_ = srcs_of(kind)
            w = 128
            k2 = k + 1
            while k2 < len(members):
                kind2, i2 = members[k2]
                hd2, _ = srcs_of(kind2)
                if hd2 is hd_ and i2 * 128 == i * 128 + w:
                    w += 128
                    k2 += 1
                else:
                    break
            out.append((hd_, ld_, i * 128, w, off))
            off += w
            k = k2
        return out

    NW = 8 * TCN

    @blk.sync
    def _(eng):
        eng.wait_ge(c.phz, PB)
        mk["cs"] = q1m.inc(eng.dma_start(out=cos_t[:], in_=c.cos_d), 0)
        mk["cs"] = q1m.inc(eng.dma_start(out=sin_t[:], in_=c.sin_d), 0)
        for i in range(3):
            mk["cs"] = q1m.inc(eng.dma_start(out=mask_t[:, i * 384:(i + 1) * 384],
                                             in_=c.mask_d[i]), 0)
        for sc in range(SCN):
            if sc >= 1:
                eng.wait_ge(t1a, 24 * sc)
            for ec in range(16):
                q1n.inc(eng.dma_start(out=xnh_s[:, ec * 512:(ec + 1) * 512],
                                      in_=c.xnh_d[ec, :, sc * 512:(sc + 1) * 512]), 0)
                mk[("xn", sc)] = q1n.inc(
                    eng.dma_start(out=xnl_s[:, ec * 512:(ec + 1) * 512],
                                  in_=c.xnl_d[ec, :, sc * 512:(sc + 1) * 512]), 0)
            for j in range(4):
                tch = sc * 4 + j
                eng.wait_ge(s1o, 8 * (tch + 1))
                eng.wait_ge(v1o, 8 * (tch + 1))
                ob = tch % 2
                sp3h = ost_h[ob][:].rearrange("p (hc t) -> p hc t", hc=16)
                sp3l = ost_l[ob][:].rearrange("p (hc t) -> p hc t", hc=16)
                q1o.inc(eng.dma_start(
                    out=c.oh_d.transpose([1, 0, 2])[:, :, tch * 128:(tch + 1) * 128],
                    in_=sp3h), ob)
                mk[("osp", tch)] = q1o.inc(eng.dma_start(
                    out=c.ol_d.transpose([1, 0, 2])[:, :, tch * 128:(tch + 1) * 128],
                    in_=sp3l), ob)

    @blk.gpsimd
    def _(eng):
        eng.wait_ge(c.phz, PB)
        eng.memset(c.dum[:, 2:3], 0.0).then_inc(c.phz, 1)

    def ocn(sc, gi, oi):
        return sc * 24 + gi * 6 + oi + 1

    @blk.scalar
    def _(eng):
        eng.wait_ge(c.phz, PB)
        for sc in range(SCN):
            for gi in range(4):
                runs = grp_runs2(gi)
                for ec in range(16):
                    gec = sc * 64 + gi * 16 + ec
                    if gec >= 2:
                        gp = gec - 2
                        if gp % 16 == 15:
                            eng.wait_ge(t1a, ((gp // 16) + 1) * 6)
                        else:
                            eng.wait_ge(t1mm, gp + 1 - (gp // 16))
                    for (hd_, ld_, c0, w, boff) in runs:
                        q1w.inc(eng.dma_start(
                            out=wp_h[gec % 2][:, boff:boff + w],
                            in_=hd_[ec * 128:(ec + 1) * 128, c0:c0 + w]), gec % 2)
                        mk[("wp", gec)] = q1w.inc(eng.dma_start(
                            out=wp_l[gec % 2][:, boff:boff + w],
                            in_=ld_[ec * 128:(ec + 1) * 128, c0:c0 + w]), gec % 2)
            def s_exp(wi):
                wp = wi % 2
                p_sb = p_sb2[wp]; smt = smt2[wp]
                eng.wait_ge(v1m, wi + 1)
                if wi >= 2:
                    eng.wait_ge(t1p, wi - 1)
                for h4 in range(4):
                    eng.activation(out=p_sb[:, h4 * 384:(h4 + 1) * 384],
                                   in_=p1q[wp][:, h4 * 512:h4 * 512 + 384],
                                   func=AF.Exp, bias=smt[:, 4 + h4:5 + h4],
                                   scale=c.c125[:],
                                   accum_out=smt[:, 8 + h4:9 + h4])
                eng.drain()
                eng.activation(out=smt[:, 12:16], in_=smt[:, 12:16],
                               func=AF.Exp, bias=0.0,
                               scale=1.0).then_inc(s1x, 1)

            def s_pnorm(wi):
                wp = wi % 2
                p_sb = p_sb2[wp]; smt = smt2[wp]
                eng.wait_ge(v1d, wi + 1)
                for h4 in range(4):
                    ii = eng.activation(out=p_sb[:, h4 * 384:(h4 + 1) * 384],
                                        in_=p_sb[:, h4 * 384:(h4 + 1) * 384],
                                        func=AF.Copy, bias=0.0,
                                        scale=smt[:, 20 + h4:21 + h4])
                    if h4 == 3:
                        ii.then_inc(s1n, 1)

            def s_pth(wi):
                wp = wi % 2
                pt_h = pt_h2[wp]
                eng.wait_ge(t1p, wi + 1)
                for kc in range(3):
                    ii = eng.activation(out=pt_h[:, kc * 512:(kc + 1) * 512],
                                        in_=p1q[wp][:, (kc + 1) * 512:(kc + 2) * 512],
                                        func=AF.Copy,
                                        bias=0.0, scale=1.0)
                    if kc == 2:
                        ii.then_inc(s1h, 1)

            def s_ohi(wi):
                wp = wi % 2
                tch, w = wi // 8, wi % 8
                eng.wait_ge(t1v, wi + 1)
                if tch >= 2 and w == 0:
                    q1o.wait(eng, mk[("osp", tch - 2)])
                orow = (w % 2) * 64
                u_ = w // 2
                eng.activation(out=ost_h[tch % 2][orow:orow + 64, u_ * 512:(u_ + 1) * 512],
                               in_=p1q[wp][orow:orow + 64, 0:512], func=AF.Copy,
                               bias=0.0, scale=1.0).then_inc(s1o, 1)

            s_exp(sc * 32)
            for wv in range(sc * 32, sc * 32 + 32):
                s_pnorm(wv)
                if wv - 1 >= sc * 32:
                    s_ohi(wv - 1)
                s_pth(wv)
                if wv + 1 < sc * 32 + 32:
                    s_exp(wv + 1)
            s_ohi(sc * 32 + 31)
        eng.activation(out=c.dum[:, 0:1], in_=c.u_t[:], func=AF.Copy, bias=0.0,
                       scale=0.0).then_inc(c.phz, 1)

    @blk.vector
    def _(eng):
        eng.wait_ge(c.phz, PB)
        eng.memset(kr_h[:], 0.0)
        eng.memset(kr_l[:], 0.0)
        eng.memset(vr_h[:], 0.0)
        eng.memset(vr_l[:], 0.0)
        for sc in range(SCN):
            cosl = cos_t[:, sc * 512:(sc + 1) * 512]
            sinl = sin_t[:, sc * 512:(sc + 1) * 512]
            if sc >= 1:
                eng.wait_ge(t1s, 32 * sc)
                kh3s = kr_h[:].rearrange("p (k x) -> p k x", k=4)[:, :, 512:768]
                kh3d = kr_h[:].rearrange("p (k x) -> p k x", k=4)[:, :, 0:256]
                kl3s = kr_l[:].rearrange("p (k x) -> p k x", k=4)[:, :, 512:768]
                kl3d = kr_l[:].rearrange("p (k x) -> p k x", k=4)[:, :, 0:256]
                eng.drain()
                eng.tensor_copy(out=kh3d, in_=kh3s)
                eng.tensor_copy(out=kl3d, in_=kl3s)
                eng.drain()
            for gi in range(4):
                for oi, (kind, idx) in enumerate(GRP[gi]):
                    n = ocn(sc, gi, oi)
                    eng.wait_ge(t1a, n)
                    if n >= 3:
                        eng.wait_ge(t1r, n - 2)
                    bias = (c.bq_t[:, idx:idx + 1] if kind == "q" else
                            (c.bk_t[:, idx:idx + 1] if kind == "k"
                             else c.bv_t[:, idx:idx + 1]))
                    eng.tensor_scalar(out=qsb[(n - 1) % 2][:], in0=acc_ap(oi),
                                      scalar1=bias, scalar2=None,
                                      op0=OP.add).then_inc(s1e, 1)
                    eng.wait_ge(t1r, n)
                    if sc >= 1:
                        eng.wait_ge(t1s, 32 * sc)
                        eng.wait_ge(t1v, 32 * sc)
                    if kind in ("q", "k"):
                        eng.drain()
                        eng.tensor_mul(out=rt1[:], in0=qsb[(n - 1) % 2][:], in1=cosl)
                        eng.tensor_mul(out=rt2[:], in0=acc_ap(oi), in1=sinl)
                        eng.drain()
                        eng.tensor_add(out=rt1[:], in0=rt1[:], in1=rt2[:])
                        eng.drain()
                        if kind == "q":
                            dst_h = qr_h[:, idx * 512:(idx + 1) * 512]
                            dst_l = qr_l[:, idx * 512:(idx + 1) * 512]
                            eng.tensor_copy(out=dst_h, in_=rt1[:])
                            eng.drain()
                            eng.tensor_sub(out=dst_l, in0=rt1[:],
                                           in1=dst_h).then_inc(v1r, 1)
                        else:
                            for jj in range(4):
                                slot = 2 + jj
                                dh = kr_h[:, idx * 768 + slot * 128:idx * 768 + slot * 128 + 128]
                                dl = kr_l[:, idx * 768 + slot * 128:idx * 768 + slot * 128 + 128]
                                eng.tensor_copy(out=dh, in_=rt1[:, jj * 128:(jj + 1) * 128])
                                eng.drain()
                                ii = eng.tensor_sub(
                                    out=dl, in0=rt1[:, jj * 128:(jj + 1) * 128], in1=dh)
                                if jj == 3:
                                    ii.then_inc(v1r, 1)
                    else:
                        for jj in range(4):
                            slot = (sc * 4 + jj) % 6
                            src = acc_ap(oi)[:, jj * 128:(jj + 1) * 128]
                            dh = vr_h[:, slot * 512 + idx * 128:slot * 512 + idx * 128 + 128]
                            dl = vr_l[:, slot * 512 + idx * 128:slot * 512 + idx * 128 + 128]
                            eng.tensor_copy(out=dh, in_=src)
                            eng.drain()
                            ii = eng.tensor_sub(out=dl, in0=src, in1=dh)
                            if jj == 3:
                                ii.then_inc(v1r, 1)
            def v_rownm(wi):
                wp = wi % 2
                smt = smt2[wp]
                w = wi % 8
                eng.wait_ge(t1s, wi + 1)
                if wi >= 2:
                    eng.wait_ge(s1x, wi - 1)
                sc3 = p1q[wp][:].rearrange("p (g x) -> p g x", g=4)
                eng.tensor_reduce(out=smt[:, 0:4], in_=sc3[:, :, 0:384],
                                  axis=AX, op=OP.max)
                eng.drain()
                eng.tensor_scalar(out=smt[:, 4:8], in0=smt[:, 0:4],
                                  scalar1=-0.125, scalar2=None, op0=OP.mult)
                eng.drain()
                eng.tensor_tensor(out=smt[:, 4:8], in0=smt[:, 4:8],
                                  in1=c.nsink[:, w * 4:w * 4 + 4],
                                  op=OP.min)
                eng.drain()
                eng.tensor_sub(out=smt[:, 12:16], in0=smt[:, 4:8],
                               in1=c.nsink[:, w * 4:w * 4 + 4]).then_inc(v1m, 1)

            def v_denom(wi):
                wp = wi % 2
                smt = smt2[wp]
                eng.wait_ge(s1x, wi + 1)
                if wi >= 2:
                    eng.wait_ge(s1n, wi - 1)
                eng.tensor_add(out=smt[:, 16:20], in0=smt[:, 8:12],
                               in1=smt[:, 12:16])
                eng.drain()
                eng.reciprocal(out=smt[:, 20:24],
                               in_=smt[:, 16:20]).then_inc(v1d, 1)

            def v_ptl(wi):
                wp = wi % 2
                pt_h = pt_h2[wp]; pt_l = pt_l2[wp]
                eng.wait_ge(t1p, wi + 1)
                eng.wait_ge(s1h, wi + 1)
                for kc in range(3):
                    ii = eng.tensor_sub(out=pt_l[:, kc * 512:(kc + 1) * 512],
                                        in0=p1q[wp][:, (kc + 1) * 512:(kc + 2) * 512],
                                        in1=pt_h[:, kc * 512:(kc + 1) * 512])
                    if kc == 2:
                        ii.then_inc(v1l, 1)

            def v_olo(wi):
                wp = wi % 2
                tch, w = wi // 8, wi % 8
                eng.wait_ge(t1v, wi + 1)
                eng.wait_ge(s1o, wi + 1)
                if tch >= 2 and w == 0:
                    q1o.wait(eng, mk[("osp", tch - 2)])
                orow = (w % 2) * 64
                u_ = w // 2
                eng.tensor_sub(out=ost_l[tch % 2][orow:orow + 64, u_ * 512:(u_ + 1) * 512],
                               in0=p1q[wp][orow:orow + 64, 0:512],
                               in1=ost_h[tch % 2][orow:orow + 64, u_ * 512:(u_ + 1) * 512]
                               ).then_inc(v1o, 1)

            v_rownm(sc * 32)
            v_denom(sc * 32)
            v_rownm(sc * 32 + 1)
            for wv in range(sc * 32, sc * 32 + 32):
                v_ptl(wv)
                if wv + 1 < sc * 32 + 32:
                    v_denom(wv + 1)
                v_olo(wv)
                if wv + 2 < sc * 32 + 32:
                    v_rownm(wv + 2)
        q1o.wait(eng, mk[("osp", TCN - 2)])
        q1o.wait(eng, mk[("osp", TCN - 1)])
        q1n.wait(eng, mk[("xn", SCN - 1)])
        q1w.wait(eng, mk[("wp", SCN * 64 - 2)])
        q1w.wait(eng, mk[("wp", SCN * 64 - 1)])
        eng.memset(c.dum[:, 1:2], 0.0).then_inc(c.phz, 1)

    @blk.tensor
    def _(eng):
        eng.wait_ge(c.phz, PB)
        q1m.wait(eng, mk["cs"])
        for sc in range(SCN):
            for gi in range(4):
                for ec in range(16):
                    gec = sc * 64 + gi * 16 + ec
                    q1w.wait(eng, mk[("wp", gec)])
                    if ec == 0:
                        q1n.wait(eng, mk[("xn", sc)])
                        for oi in range(6):
                            if sc == 0 and gi == 0:
                                break
                            if gi >= 1:
                                n_prev = ocn(sc, gi - 1, oi)
                            else:
                                n_prev = ocn(sc - 1, 3, oi)
                            eng.wait_ge(s1e, n_prev)
                            eng.wait_ge(v1r, n_prev)
                        if sc >= 1 and gi == 0:
                            eng.wait_ge(s1x, 32 * sc)
                            eng.wait_ge(v1m, 32 * sc)
                            eng.wait_ge(s1h, 32 * sc)
                            eng.wait_ge(v1l, 32 * sc)
                            eng.wait_ge(s1o, 32 * sc)
                            eng.wait_ge(v1o, 32 * sc)
                    xh = xnh_s[:, ec * 512:(ec + 1) * 512]
                    xl = xnl_s[:, ec * 512:(ec + 1) * 512]
                    for oi in range(6):
                        lh = wp_h[gec % 2][:, oi * 128:(oi + 1) * 128]
                        ll = wp_l[gec % 2][:, oi * 128:(oi + 1) * 128]
                        st = (ec == 0)
                        sp = (ec == 15)
                        a = acc_ap(oi)
                        eng.matmul(a, lh, xh, start=st, stop=False)
                        eng.matmul(a, lh, xl, start=False, stop=False)
                        ii = eng.matmul(a, ll, xh, start=False, stop=sp)
                        if sp:
                            ii.then_inc(t1a, 1)
                        elif oi == 5:
                            ii.then_inc(t1mm, 1)
                for oi, (kind, idx) in enumerate(GRP[gi]):
                    n = ocn(sc, gi, oi)
                    eng.wait_ge(s1e, n)
                    if kind in ("q", "k"):
                        eng.matmul(acc_ap(oi), c.RT_t[:], qsb[(n - 1) % 2][:],
                                   start=True, stop=True).then_inc(t1r, 1)
                    else:
                        a = acc_ap(oi)
                        for jj in range(4):
                            ii = eng.transpose(
                                a[:, jj * 128:(jj + 1) * 128],
                                qsb[(n - 1) % 2][:, jj * 128:(jj + 1) * 128],
                                c.idf_t[:])
                            if jj == 3:
                                ii.then_inc(t1r, 1)
            def emit_scores(tch, w):
                tca = tch
                wi = tch * 8 + w
                msel = 0 if tca == 0 else (1 if tca == 1 else 2)
                gkv = w
                kvc = gkv // 2
                prow = (gkv % 2) * 64
                if wi == sc * 32:
                    eng.wait_ge(v1r, 24 * (sc + 1))
                if wi >= 2:
                    eng.wait_ge(s1h, wi - 1)
                    eng.wait_ge(v1l, wi - 1)
                    eng.wait_ge(s1o, wi - 1)
                    eng.wait_ge(v1o, wi - 1)
                for h4 in range(4):
                    blkq = 4 * (w // 2) + h4
                    qrow = (w % 2) * 64
                    qcol = blkq * 512 + (tca % 4) * 128
                    kco = kvc * 768 + (tca % 4) * 128
                    qh = qr_h[qrow:qrow + 64, qcol:qcol + 128]
                    ql = qr_l[qrow:qrow + 64, qcol:qcol + 128]
                    kh = kr_h[prow:prow + 64, kco:kco + 384]
                    kl = kr_l[prow:prow + 64, kco:kco + 384]
                    o_ = p1q[wi % 2][:, h4 * 512:h4 * 512 + 384]
                    eng.matmul(o_, c.idb_t[:],
                               mask_t[:, msel * 384:(msel + 1) * 384],
                               start=True, stop=False)
                    eng.matmul(o_, qh, kh, start=False, stop=False)
                    eng.matmul(o_, qh, kl, start=False, stop=False)
                    ii = eng.matmul(o_, ql, kh, start=False, stop=True)
                    if h4 == 3:
                        ii.then_inc(t1s, 1)

            def emit_pt(tch, w):
                wi = tch * 8 + w
                wp = wi % 2
                p_sb = p_sb2[wp]
                eng.wait_ge(s1n, wi + 1)
                for h4 in range(4):
                    for kc in range(3):
                        ii = eng.transpose(
                            p1q[wp][:, (kc + 1) * 512 + h4 * 128:
                                    (kc + 1) * 512 + (h4 + 1) * 128],
                            p_sb[:, h4 * 384 + kc * 128:h4 * 384 + (kc + 1) * 128],
                            c.idf_t[:])
                        if h4 == 3 and kc == 2:
                            ii.then_inc(t1p, 1)

            def emit_av(tch, w):
                wi = tch * 8 + w
                tca = tch
                gkv = w
                wp = wi % 2
                pt_h = pt_h2[wp]; pt_l = pt_l2[wp]
                eng.wait_ge(s1h, wi + 1)
                eng.wait_ge(v1l, wi + 1)
                if wi >= 1:
                    eng.wait_ge(s1o, wi - 1)
                    eng.wait_ge(v1o, wi - 1)
                orow = (w % 2) * 64
                for kc in range(3):
                    slot = (tca - 2 + kc) % 6
                    vco = slot * 512 + gkv * 64
                    vh = vr_h[:, vco:vco + 64]
                    vl = vr_l[:, vco:vco + 64]
                    ph_ = pt_h[:, kc * 512:(kc + 1) * 512]
                    pl_ = pt_l[:, kc * 512:(kc + 1) * 512]
                    o_ = p1q[wp][orow:orow + 64, 0:512]
                    eng.matmul(o_, vh, ph_, start=(kc == 0), stop=False)
                    eng.matmul(o_, vh, pl_, start=False, stop=False)
                    ii = eng.matmul(o_, vl, ph_, start=False, stop=(kc == 2))
                    if kc == 2:
                        ii.then_inc(t1v, 1)

            waves_sc = [(sc * 4 + j, w) for j in range(4) for w in range(8)]
            emit_scores(*waves_sc[0])
            for iw, (tch_, w_) in enumerate(waves_sc):
                if iw + 1 < len(waves_sc):
                    emit_scores(*waves_sc[iw + 1])
                emit_pt(tch_, w_)
                emit_av(tch_, w_)
        NW_ = 8 * TCN
        eng.wait_ge(s1h, NW_)
        eng.wait_ge(v1l, NW_)
        eng.wait_ge(s1o, NW_)
        eng.wait_ge(v1o, NW_)
        eng.transpose(p1q[0][:, 0:128], c.idf_t[:], c.idf_t[:]).then_inc(c.phz, 1)

    es.close()


def _phase15(c):
    nc, blk, T, TCN = c.nc, c.blk, c.T, c.TCN
    es = ExitStack()
    sb = lambda n, s, d: es.enter_context(nc.sbuf_tensor(n, s, d))
    psm = lambda n, s, d=F32: es.enter_context(nc.psum_tensor(n, s, d))
    PB = 4 * c.PH

    woh_s = sb("woh_s", [128, 16 * 2048], BF16)
    wol_s = sb("wol_s", [128, 16 * 2048], BF16)
    osb_h = [sb("osb_h0", [128, 2048], BF16), sb("osb_h1", [128, 2048], BF16)]
    osb_l = [sb("osb_l0", [128, 2048], BF16), sb("osb_l1", [128, 2048], BF16)]
    xb15 = sb("xb15", [128, 2048], F32)
    ty = [sb("ty0", [128, 2048], F32), sb("ty1", [128, 2048], F32)]
    bo_sb = sb("bo_sb", [1, 2048], F32)
    bo_b = sb("bo_b", [128, 2048], F32)
    wsb = [sb("wsb0", [128, 2048], F32), sb("wsb1", [128, 2048], F32)]
    p15 = [psm("p15_0", [128, 2048], F32), psm("p15_1", [128, 2048], F32)]

    q15w = c.DQ("q15w", 1)
    q15b = c.DQ("q15b", 1)
    q15o = c.DQ("q15o", 2)
    q15x = c.DQ("q15x", 1)
    q15y = c.DQ("q15y", 2)
    t15 = c.sem("t15"); v15 = c.sem("v15"); t15b = c.sem("t15b"); v15b = c.sem("v15b")
    qZ = c.DQ("qZ", 1)
    qY = c.DQ("qY", 1)
    v1z = c.sem("v1z"); s1z = c.sem("s1z")
    mk = {}
    SJ = []
    for wi_, wd_ in enumerate([c.wgT_d, c.wuT_d]):
        for r0 in range(0, E, 128):
            for cq in range(4):
                SJ.append((wd_, r0, cq * 2048, 2048, 64 * wi_ + (r0 // 128) * 4 + cq))
    for r0 in range(0, II, 128):
        SJ.append((c.wdT_d, r0, 0, 2048, 128 + r0 // 128))
    NSJ = len(SJ)
    SJV = [j for j in range(NSJ) if j % 2 == 0]
    SJS = [j for j in range(NSJ) if j % 2 == 1]

    @blk.sync
    def _(eng):
        eng.wait_ge(c.phz, PB)
        mk["bo"] = q15b.inc(eng.dma_start(out=bo_sb[:], in_=c.bo_d), 0)
        for hdc in range(16):
            q15w.inc(eng.dma_start(out=woh_s[:, hdc * 2048:(hdc + 1) * 2048],
                                   in_=c.woh_d[hdc * 128:(hdc + 1) * 128, :]), 0)
            mk["wo"] = q15w.inc(eng.dma_start(out=wol_s[:, hdc * 2048:(hdc + 1) * 2048],
                                              in_=c.wol_d[hdc * 128:(hdc + 1) * 128, :]), 0)
        for tch in range(TCN):
            if tch >= 2:
                eng.wait_ge(t15, tch - 1)
            ob = tch % 2
            oh3 = osb_h[ob][:].rearrange("p (hc t) -> p hc t", hc=16)
            ol3 = osb_l[ob][:].rearrange("p (hc t) -> p hc t", hc=16)
            q15o.inc(eng.dma_start(
                out=oh3, in_=c.oh_d.transpose([1, 0, 2])[:, :, tch * 128:(tch + 1) * 128]), ob)
            mk[("o", tch)] = q15o.inc(eng.dma_start(
                out=ol3, in_=c.ol_d.transpose([1, 0, 2])[:, :, tch * 128:(tch + 1) * 128]), ob)
            if tch >= 1:
                eng.wait_ge(v15, tch)
            mk[("x", tch)] = q15x.inc(eng.dma_start(
                out=xb15[:], in_=c.x_d[tch * 128:(tch + 1) * 128, :]), 0)
            if tch >= 1:
                tp = tch - 1
                eng.wait_ge(v15, tp + 1)
                mk[("ysp", tp)] = q15y.inc(eng.dma_start(
                    out=c.ysp_d[tp * 128:(tp + 1) * 128, :], in_=ty[tp % 2][:]), tp % 2)
            nsjs = len(SJS)
            for k in range(nsjs * tch // TCN, nsjs * (tch + 1) // TCN):
                j = SJS[k]
                (wd_, r0, c0, w, col) = SJ[j]
                if k >= 1:
                    eng.wait_ge(s1z, k)
                mk[("z", j)] = qY.inc(eng.dma_start(
                    out=wsb[1][:, 0:w], in_=wd_[r0:r0 + 128, c0:c0 + w]), 0)
        tp = TCN - 1
        eng.wait_ge(v15, tp + 1)
        mk[("ysp", tp)] = q15y.inc(eng.dma_start(
            out=c.ysp_d[tp * 128:(tp + 1) * 128, :], in_=ty[tp % 2][:]), tp % 2)

    @blk.gpsimd
    def _(eng):
        eng.wait_ge(c.phz, PB)
        for k, j in enumerate(SJV):
            (wd_, r0, c0, w, col) = SJ[j]
            if k >= 1:
                eng.wait_ge(v1z, k)
            mk[("z", j)] = qZ.inc(eng.dma_start(out=wsb[0][:, 0:w],
                                                in_=wd_[r0:r0 + 128, c0:c0 + w]),
                                  0)
        eng.memset(c.dum[:, 2:3], 0.0).then_inc(c.phz, 1)


    @blk.tensor
    def _(eng):
        eng.wait_ge(c.phz, PB)
        q15b.wait(eng, mk["bo"])
        for eb in range(4):
            eng.matmul(p15[0][:, eb * 512:(eb + 1) * 512], c.on1_t[:],
                       bo_sb[0:1, eb * 512:(eb + 1) * 512], start=True,
                       stop=True).then_inc(t15b, 1)
        q15w.wait(eng, mk["wo"])
        for tch in range(TCN):
            pp = p15[tch % 2]
            q15o.wait(eng, mk[("o", tch)])
            if tch == 0:
                eng.wait_ge(v15b, 1)
            if tch >= 2:
                eng.wait_ge(v15, tch - 1)
            for hdc in range(16):
                lh = osb_h[tch % 2][:, hdc * 128:(hdc + 1) * 128]
                ll = osb_l[tch % 2][:, hdc * 128:(hdc + 1) * 128]
                for eb in range(4):
                    a = pp[:, eb * 512:(eb + 1) * 512]
                    wh = woh_s[:, hdc * 2048 + eb * 512:hdc * 2048 + (eb + 1) * 512]
                    wl = wol_s[:, hdc * 2048 + eb * 512:hdc * 2048 + (eb + 1) * 512]
                    eng.matmul(a, lh, wh, start=(hdc == 0), stop=False)
                    eng.matmul(a, lh, wl, start=False, stop=False)
                    ii = eng.matmul(a, ll, wh, start=False, stop=(hdc == 15))
                    if hdc == 15 and eb == 3:
                        ii.then_inc(t15, 1)
        eng.wait_ge(v15, TCN)
        eng.transpose(p15[0][:, 0:128], c.idf_t[:], c.idf_t[:]).then_inc(c.phz, 1)

    @blk.vector
    def _(eng):
        eng.wait_ge(c.phz, PB)
        eng.wait_ge(t15b, 4)
        eng.tensor_copy(out=bo_b[:], in_=p15[0][:]).then_inc(v15b, 1)
        for tch in range(TCN):
            eng.wait_ge(t15, tch + 1)
            q15x.wait(eng, mk[("x", tch)])
            if tch >= 2:
                q15y.wait(eng, mk[("ysp", tch - 2)])
            eng.tensor_add(out=ty[tch % 2][:], in0=p15[tch % 2][:], in1=bo_b[:])
            eng.drain()
            eng.tensor_add(out=ty[tch % 2][:], in0=ty[tch % 2][:],
                           in1=xb15[:]).then_inc(v15, 1)
            nsjv = len(SJV)
            for k in range(nsjv * tch // TCN, nsjv * (tch + 1) // TCN):
                j = SJV[k]
                (wd_, r0, c0, w, col) = SJ[j]
                qZ.wait(eng, mk[("z", j)])
                eng.tensor_reduce(out=c.wsum[:, col:col + 1], in_=wsb[0][:, 0:w],
                                  axis=AX, op=OP.add,
                                  apply_absolute_value=True).then_inc(v1z, 1)
        q15y.wait(eng, mk[("ysp", TCN - 2)])
        q15y.wait(eng, mk[("ysp", TCN - 1)])
        eng.memset(c.dum[:, 1:2], 0.0).then_inc(c.phz, 1)

    @blk.scalar
    def _(eng):
        eng.wait_ge(c.phz, PB)
        for j in SJS:
            (wd_, r0, c0, w, col) = SJ[j]
            qY.wait(eng, mk[("z", j)])
            eng.activation(out=wsb[1][:, 0:w], in_=wsb[1][:, 0:w],
                           func=AF.Abs, bias=0.0, scale=1.0,
                           accum_out=c.wsum[:, col:col + 1]).then_inc(s1z, 1)
        eng.activation(out=c.dum[:, 0:1], in_=c.u_t[:], func=AF.Copy, bias=0.0,
                       scale=0.0).then_inc(c.phz, 1)

    es.close()


def _phase2(c):
    nc, blk, T, TCN = c.nc, c.blk, c.T, c.TCN
    es = ExitStack()
    sb = lambda n, s, d: es.enter_context(nc.sbuf_tensor(n, s, d))
    psm = lambda n, s, d=F32: es.enter_context(nc.psum_tensor(n, s, d))
    PB = 4 * c.PH
    IBN = 16

    xqf = sb("xqf", [128, 16 * T], BF16)
    wgp = [sb("wgp0", [128, 512], F32), sb("wgp1", [128, 512], F32)]
    wup = [sb("wup0", [128, 512], F32), sb("wup1", [128, 512], F32)]
    sg1 = [sb("sg1_0", [128, 512], F32), sb("sg1_1", [128, 512], F32)]
    sg2 = [sb("sg2_0", [128, 512], F32), sb("sg2_1", [128, 512], F32)]
    su1 = [sb("su1_0", [128, 512], F32), sb("su1_1", [128, 512], F32)]
    su2 = [sb("su2_0", [128, 512], F32), sb("su2_1", [128, 512], F32)]
    wgq = [sb("wgq0", [128, 16 * 512], BF16), sb("wgq1", [128, 16 * 512], BF16)]
    wuq = [sb("wuq0", [128, 16 * 512], BF16), sb("wuq1", [128, 16 * 512], BF16)]
    gact = [sb("gact0", [128, 512], F32), sb("gact1", [128, 512], F32)]
    zt = [sb("zt0", [128, 512], F32), sb("zt1", [128, 512], F32)]
    sm2 = sb("sm2", [128, 4], F32)
    ws3 = sb("ws3", [128, 4], F32)
    sb3 = sb("sb3", [128, 4], F32)
    row3 = sb("row3", [1, 4], F32)
    p2 = [psm("p2_0", [128, 512], F32), psm("p2_1", [128, 512], F32),
          psm("p2_2", [128, 512], F32), psm("p2_3", [128, 512], F32)]

    q2x = c.DQ("q2x", 1)
    q2w = c.DQ("q2w", 2)
    q2z = c.DQ("q2z", 2)
    t2 = c.sem("t2"); s2g = c.sem("s2g"); v2z = c.sem("v2z")
    s2q = c.sem("s2q"); v2q = c.sem("v2q"); g2q = c.sem("g2q")
    t2s = c.sem("t2s"); v2s = c.sem("v2s")
    mk = {}

    @blk.sync
    def _(eng):
        eng.wait_ge(c.phz, PB)
        for ec in range(16):
            mk["xq"] = q2x.inc(eng.dma_start(out=xqf[:, ec * T:(ec + 1) * T],
                                             in_=c.xq_d[ec, :, :]), 0)
        for ib in range(IBN):
            for tch in range(TCN):
                idx = ib * TCN + tch
                eng.wait_ge(v2z, idx + 1)
                mk[("z", idx)] = q2z.inc(eng.dma_start(
                    out=c.z_d[tch * 128:(tch + 1) * 128, ib * 512:(ib + 1) * 512],
                    in_=zt[idx % 2][:]), idx % 2)

    @blk.scalar
    def _(eng):
        eng.wait_ge(c.phz, PB)
        for ib in range(IBN + 1):
            for j in range(16):
                if ib < IBN:
                    idx = ib * 16 + j
                    if idx >= 2:
                        eng.wait_ge(s2q, idx - 1)
                        eng.wait_ge(v2q, idx - 1)
                    q2w.inc(eng.dma_start(
                        out=wgp[idx % 2][:, 0:512],
                        in_=c.wgT_d[j * 128:(j + 1) * 128, ib * 512:(ib + 1) * 512]),
                        idx % 2)
                    mk[("w", idx)] = q2w.inc(eng.dma_start(
                        out=wup[idx % 2][:, 0:512],
                        in_=c.wuT_d[j * 128:(j + 1) * 128, ib * 512:(ib + 1) * 512]),
                        idx % 2)
                    if idx == 0:
                        eng.wait_ge(v2s, 4)
                    q2w.wait(eng, mk[("w", idx)])
                    if idx >= 2:
                        eng.wait_ge(g2q, idx - 1)
                    eng.activation(out=sg1[idx % 2][:], in_=wgp[idx % 2][:, 0:512],
                                   func=AF.Sign, bias=c.wsc[:, 1:2], scale=1.0)
                    eng.activation(out=sg2[idx % 2][:], in_=wgp[idx % 2][:, 0:512],
                                   func=AF.Sign, bias=c.wsc[:, 0:1],
                                   scale=1.0).then_inc(s2q, 1)
                if ib >= 1 and j < TCN:
                    ibp, tch = ib - 1, j
                    idz = ibp * TCN + tch
                    eng.wait_ge(t2, idz + 1)
                    if idz >= 2:
                        eng.wait_ge(v2z, idz - 1)
                    eng.activation(out=gact[idz % 2][:], in_=p2[2 * (idz % 2)][:],
                                   func=AF.Silu, bias=0.0,
                                   scale=c.xswg[:, tch:tch + 1]).then_inc(s2g, 1)
        eng.activation(out=c.dum[:, 0:1], in_=c.u_t[:], func=AF.Copy, bias=0.0,
                       scale=0.0).then_inc(c.phz, 1)

    @blk.gpsimd
    def _(eng):
        eng.wait_ge(c.phz, PB)
        for ib in range(IBN):
            if ib >= 2:
                eng.wait_ge(t2, (ib - 1) * TCN)
            for j in range(16):
                idx = ib * 16 + j
                eng.wait_ge(s2q, idx + 1)
                eng.wait_ge(v2q, idx + 1)
                eng.tensor_add(out=wgq[ib % 2][:, j * 512:(j + 1) * 512],
                               in0=sg1[idx % 2][:], in1=sg2[idx % 2][:])
                eng.drain()
                eng.tensor_sub(out=wuq[ib % 2][:, j * 512:(j + 1) * 512],
                               in0=su1[idx % 2][:], in1=su2[idx % 2][:])
                eng.drain()
                eng.tensor_copy(out=c.dum[:, 3:4], in_=c.u_t[:]).then_inc(g2q, 1)
        eng.memset(c.dum[:, 2:3], 0.0).then_inc(c.phz, 1)

    @blk.vector
    def _(eng):
        eng.wait_ge(c.phz, PB)
        eng.tensor_reduce(out=ws3[:, 0:1], in_=c.wsum[:, 0:64], axis=AX, op=OP.add)
        eng.tensor_reduce(out=ws3[:, 1:2], in_=c.wsum[:, 64:128], axis=AX, op=OP.add)
        eng.tensor_reduce(out=ws3[:, 2:3], in_=c.wsum[:, 128:192], axis=AX, op=OP.add)
        eng.memset(ws3[:, 3:4], 1.0)
        eng.drain()
        eng.tensor_copy(out=c.dum[:, 4:5], in_=ws3[:, 0:1]).then_inc(v2s, 1)
        eng.wait_ge(t2s, 1)
        eng.tensor_copy(out=sb3[0:3, 0:1], in_=p2[0][0:3, 0:1]).then_inc(v2s, 1)
        eng.wait_ge(t2s, 2)
        eng.tensor_copy(out=row3[0:1, 0:3], in_=p2[1][0:1, 0:3]).then_inc(v2s, 1)
        eng.wait_ge(t2s, 3)
        eng.tensor_scalar(out=c.wsc[:, 0:1], in0=p2[2][:, 0:1],
                          scalar1=float(2.0 ** -25), scalar2=5e-6,
                          op0=OP.mult, op1=OP.add)
        eng.tensor_scalar(out=c.wsc[:, 2:3], in0=p2[2][:, 1:2],
                          scalar1=float(2.0 ** -25), scalar2=5e-6,
                          op0=OP.mult, op1=OP.add)
        eng.tensor_scalar(out=c.wsc[:, 4:5], in0=p2[2][:, 2:3],
                          scalar1=float(2.0 ** -25), scalar2=5e-6,
                          op0=OP.mult, op1=OP.add)
        eng.drain()
        eng.tensor_scalar(out=c.wsc[:, 1:2], in0=c.wsc[:, 0:1], scalar1=-1.0,
                          scalar2=None, op0=OP.mult)
        eng.tensor_scalar(out=c.wsc[:, 3:4], in0=c.wsc[:, 2:3], scalar1=-1.0,
                          scalar2=None, op0=OP.mult)
        eng.tensor_scalar(out=c.wsc[:, 5:6], in0=c.wsc[:, 4:5], scalar1=-1.0,
                          scalar2=None, op0=OP.mult)
        eng.drain()
        eng.tensor_scalar(out=c.xswg[:], in0=c.xs_all[:], scalar1=c.wsc[:, 0:1],
                          scalar2=None, op0=OP.mult)
        eng.tensor_scalar(out=c.xswu[:], in0=c.xs_all[:], scalar1=c.wsc[:, 2:3],
                          scalar2=2.0, op0=OP.mult, op1=OP.mult)
        eng.drain()
        eng.tensor_copy(out=c.dum[:, 5:6], in_=c.xswg[:, 0:1]).then_inc(v2s, 1)
        for ib in range(IBN + 1):
            for j in range(16):
                if ib < IBN:
                    idx = ib * 16 + j
                    q2w.wait(eng, mk[("w", idx)])
                    if idx >= 2:
                        eng.wait_ge(g2q, idx - 1)
                    eng.tensor_scalar(out=su1[idx % 2][:], in0=wup[idx % 2][:, 0:512],
                                      scalar1=c.wsc[:, 2:3], scalar2=None,
                                      op0=OP.is_ge)
                    eng.tensor_scalar(out=su2[idx % 2][:], in0=wup[idx % 2][:, 0:512],
                                      scalar1=c.wsc[:, 3:4], scalar2=None,
                                      op0=OP.is_le).then_inc(v2q, 1)
                if ib >= 1 and j < TCN:
                    ibp, tch = ib - 1, j
                    idz = ibp * TCN + tch
                    eng.wait_ge(s2g, idz + 1)
                    if idz >= 2:
                        q2z.wait(eng, mk[("z", idz - 2)])
                    eng.scalar_tensor_tensor(out=zt[idz % 2][:], in0=gact[idz % 2][:],
                                             scalar=c.xswu[:, tch:tch + 1],
                                             in1=p2[2 * (idz % 2) + 1][:],
                                             op0=OP.mult, op1=OP.mult)
                    eng.drain()
                    eng.tensor_reduce(out=sm2[:, 0:1], in_=zt[idz % 2][:], axis=AX,
                                      op=OP.max, apply_absolute_value=True)
                    eng.drain()
                    eng.tensor_tensor(out=c.rmax_z[:, tch:tch + 1],
                                      in0=c.rmax_z[:, tch:tch + 1], in1=sm2[:, 0:1],
                                      op=OP.max).then_inc(v2z, 1)
        eng.drain()
        eng.tensor_scalar(out=c.s2i_all[:], in0=c.rmax_z[:], scalar1=c.r127[:],
                          scalar2=1e-5, op0=OP.mult, op1=OP.add)
        eng.drain()
        eng.tensor_scalar(out=c.s2w_all[:], in0=c.s2i_all[:],
                          scalar1=c.wsc[:, 4:5], scalar2=None, op0=OP.mult)
        eng.drain()
        eng.reciprocal(out=c.s2i_all[:], in_=c.s2i_all[:])
        q2z.wait(eng, mk[("z", IBN * TCN - 2)])
        q2z.wait(eng, mk[("z", IBN * TCN - 1)])
        q2x.wait(eng, mk["xq"])
        eng.memset(c.dum[:, 1:2], 0.0).then_inc(c.phz, 1)

    @blk.tensor
    def _(eng):
        eng.wait_ge(c.phz, PB)
        eng.wait_ge(v2s, 1)
        eng.matmul(p2[0][0:3, 0:1], ws3[:, 0:3], ws3[:, 3:4], start=True,
                   stop=True).then_inc(t2s, 1)
        eng.wait_ge(v2s, 2)
        eng.transpose(p2[1][0:1, 0:3], sb3[0:3, 0:1],
                      c.idf_t[0:3, 0:3]).then_inc(t2s, 1)
        eng.wait_ge(v2s, 3)
        eng.matmul(p2[2][:, 0:3], c.on1_t[:], row3[0:1, 0:3], start=True,
                   stop=True).then_inc(t2s, 1)
        q2x.wait(eng, mk["xq"])
        for ib in range(IBN):
            eng.wait_ge(g2q, 16 * (ib + 1))
            for tch in range(TCN):
                idx = ib * TCN + tch
                if idx >= 2:
                    eng.wait_ge(s2g, idx - 1)
                    eng.wait_ge(v2z, idx - 1)
                elif ib == 0 and tch == 0:
                    eng.wait_ge(v2s, 3)
                pg = p2[2 * (idx % 2)]
                pu = p2[2 * (idx % 2) + 1]
                for ec in range(16):
                    lt = xqf[:, ec * T + tch * 128:ec * T + (tch + 1) * 128]
                    eng.matmul(pg[:], lt, wgq[ib % 2][:, ec * 512:(ec + 1) * 512],
                               start=(ec == 0), stop=(ec == 15))
                    ii = eng.matmul(pu[:], lt, wuq[ib % 2][:, ec * 512:(ec + 1) * 512],
                                    start=(ec == 0), stop=(ec == 15))
                    if ec == 15:
                        ii.then_inc(t2, 1)
        eng.wait_ge(s2g, IBN * TCN)
        eng.wait_ge(v2z, IBN * TCN)
        eng.transpose(p2[0][:, 0:128], c.idf_t[:], c.idf_t[:]).then_inc(c.phz, 1)

    es.close()


def _phase3(c):
    nc, blk, T, TCN = c.nc, c.blk, c.T, c.TCN
    es = ExitStack()
    sb = lambda n, s, d: es.enter_context(nc.sbuf_tensor(n, s, d))
    psm = lambda n, s, d=F32: es.enter_context(nc.psum_tensor(n, s, d))
    PB = 4 * c.PH
    NQ = 4

    wdq = sb("wdq", [128, 64 * 512], BF16)
    wdp = [sb("wdp0", [128, 512], F32), sb("wdp1", [128, 512], F32)]
    sd1 = [sb("sd1_0", [128, 512], F32), sb("sd1_1", [128, 512], F32)]
    sd2 = [sb("sd2_0", [128, 512], F32), sb("sd2_1", [128, 512], F32)]
    zp = [sb("zp0", [128, 2048], F32), sb("zp1", [128, 2048], F32)]
    vv3 = [sb("vv3a", [128, 2048], F32), sb("vv3b", [128, 2048], F32)]
    zqp = [sb("zqp0", [128, 2048], BF16), sb("zqp1", [128, 2048], BF16)]
    zqf = [sb("zqf0", [128, 64 * 128], BF16), sb("zqf1", [128, 64 * 128], BF16)]
    t3sb = [sb("t3sb0", [128, 512], F32), sb("t3sb1", [128, 512], F32)]
    outb = [sb("outb0", [128, 512], F32), sb("outb1", [128, 512], F32)]
    ybuf = [sb("ybuf0", [128, 512], F32), sb("ybuf1", [128, 512], F32)]
    p3 = [psm("p3_0", [128, 512], F32), psm("p3_1", [128, 512], F32)]
    p3t = [psm("p3t0", [128, 1024], BF16), psm("p3t1", [128, 1024], BF16)]

    q3w = c.DQ("q3w", 2)
    q3z = c.DQ("q3z", 2)
    q3q = c.DQ("q3q", 2)
    q3s = c.DQ("q3s", 2)
    q3y = c.DQ("q3y", 2)
    qO = c.DQ("qO", 2)
    t3 = c.sem("t3"); t3f = c.sem("t3f"); t3t = c.sem("t3t")
    s3v = c.sem("s3v"); g3 = c.sem("g3"); v3q = c.sem("v3q")
    v3e = c.sem("v3e")
    s3q = c.sem("s3q"); g3w = c.sem("g3w")
    s3e = c.sem("s3e"); v3y = c.sem("v3y")
    mk = {}

    @blk.sync
    def _(eng):
        eng.wait_ge(c.phz, PB)
        for q in range(NQ):
            for tch in range(TCN):
                idx = q * TCN + tch
                if q == 0:
                    for p in range(4):
                        pidx = tch * 4 + p
                        if pidx >= 2:
                            eng.wait_ge(s3v, pidx - 1)
                        mk[("zl", pidx)] = q3z.inc(eng.dma_start(
                            out=zp[pidx % 2][:],
                            in_=c.z_d[tch * 128:(tch + 1) * 128,
                                      p * 2048:(p + 1) * 2048]), pidx % 2)
                    if tch >= 1:
                        eng.wait_ge(v3e, 8 * tch)
                        zq3 = zqf[(tch - 1) % 2][:].rearrange(
                            "p (ic t) -> p ic t", ic=64)
                        mk[("zs", tch - 1)] = q3s.inc(eng.dma_start(
                            out=c.zq_d.transpose([1, 0, 2])[
                                :, :, (tch - 1) * 128:tch * 128],
                            in_=zq3), (tch - 1) % 2)
                else:
                    if q == 1 and tch == 0:
                        tchl = TCN - 1
                        eng.wait_ge(v3e, 8 * TCN)
                        zq3l = zqf[tchl % 2][:].rearrange("p (ic t) -> p ic t", ic=64)
                        mk[("zs", tchl)] = q3s.inc(eng.dma_start(
                            out=c.zq_d.transpose([1, 0, 2])[
                                :, :, tchl * 128:(tchl + 1) * 128],
                            in_=zq3l), tchl % 2)
                        q3s.wait(eng, mk[("zs", TCN - 2)])
                        q3s.wait(eng, mk[("zs", TCN - 1)])
                    if True:
                        if tch >= 2 or q >= 2:
                            eng.wait_ge(t3, idx - 1)
                        mk[("zq", idx)] = q3q.inc(eng.dma_start(
                            out=zqf[tch % 2][:].rearrange("p (ic t) -> p ic t",
                                                          ic=64),
                            in_=c.zq_d.transpose([1, 0, 2])[
                                :, :, tch * 128:(tch + 1) * 128]), tch % 2)
                if idx >= 2:
                    eng.wait_ge(v3y, idx - 1)
                mk[("y", idx)] = q3y.inc(eng.dma_start(
                    out=ybuf[idx % 2][:],
                    in_=c.ysp_d[tch * 128:(tch + 1) * 128,
                                q * 512:(q + 1) * 512]), idx % 2)
                if idx >= 1:
                    pi = idx - 1
                    qq, tt = pi // TCN, pi % TCN
                    eng.wait_ge(v3y, pi + 1)
                    mk[("o", pi)] = qO.inc(eng.dma_start(
                        out=c.y_d[tt * 128:(tt + 1) * 128, qq * 512:(qq + 1) * 512],
                        in_=outb[pi % 2][:]), pi % 2)
        pi = NQ * TCN - 1
        qq, tt = pi // TCN, pi % TCN
        eng.wait_ge(v3y, pi + 1)
        mk[("o", pi)] = qO.inc(eng.dma_start(
            out=c.y_d[tt * 128:(tt + 1) * 128, qq * 512:(qq + 1) * 512],
            in_=outb[pi % 2][:]), pi % 2)
        qO.wait(eng, mk[("o", NQ * TCN - 2)])
        qO.wait(eng, mk[("o", NQ * TCN - 1)])

    @blk.scalar
    def _(eng):
        eng.wait_ge(c.phz, PB)
        for q in range(NQ):
            for ic in range(64):
                jdx = q * 64 + ic
                if jdx >= 2:
                    eng.wait_ge(s3q, jdx - 1)
                mk[("wd", jdx)] = q3w.inc(eng.dma_start(
                    out=wdp[jdx % 2][:],
                    in_=c.wdT_d[ic * 128:(ic + 1) * 128, q * 512:(q + 1) * 512]),
                    jdx % 2)
                q3w.wait(eng, mk[("wd", jdx)])
                if jdx >= 2:
                    eng.wait_ge(g3w, jdx - 1)
                eng.activation(out=sd1[jdx % 2][:], in_=wdp[jdx % 2][:],
                               func=AF.Sign, bias=c.wsc[:, 5:6], scale=1.0)
                eng.activation(out=sd2[jdx % 2][:], in_=wdp[jdx % 2][:],
                               func=AF.Sign, bias=c.wsc[:, 4:5],
                               scale=1.0).then_inc(s3q, 1)
            for tch in range(TCN):
                idx = q * TCN + tch
                if q == 0:
                    for p in range(4):
                        pidx = tch * 4 + p
                        q3z.wait(eng, mk[("zl", pidx)])
                        if pidx >= 2:
                            eng.wait_ge(v3q, pidx - 1)
                        eng.activation(out=vv3[pidx % 2][:], in_=zp[pidx % 2][:],
                                       func=AF.Copy, bias=0.0,
                                       scale=c.s2i_all[:, tch:tch + 1]
                                       ).then_inc(s3v, 1)
                eng.wait_ge(t3, idx + 1)
                if idx >= 2:
                    eng.wait_ge(v3y, idx - 1)
                eng.activation(out=t3sb[idx % 2][:], in_=p3[idx % 2][:],
                               func=AF.Copy, bias=0.0,
                               scale=c.s2w_all[:, tch:tch + 1]).then_inc(s3e, 1)
        eng.activation(out=c.dum[:, 0:1], in_=c.u_t[:], func=AF.Copy, bias=0.0,
                       scale=0.0).then_inc(c.phz, 1)

    @blk.gpsimd
    def _(eng):
        eng.wait_ge(c.phz, PB)
        for q in range(NQ):
            for ic in range(64):
                jdx = q * 64 + ic
                eng.wait_ge(s3q, jdx + 1)
                if q >= 1:
                    icg = ic // 8
                    eng.wait_ge(t3f, 8 * (q - 1) + icg + 1)
                eng.tensor_add(out=wdq[:, ic * 512:(ic + 1) * 512],
                               in0=sd1[jdx % 2][:], in1=sd2[jdx % 2][:])
                eng.drain()
                eng.tensor_copy(out=c.dum[:, 3:4], in_=c.u_t[:]).then_inc(g3w, 1)
        eng.memset(c.dum[:, 2:3], 0.0).then_inc(c.phz, 1)

    @blk.vector
    def _(eng):
        eng.wait_ge(c.phz, PB)
        for q in range(NQ):
            for tch in range(TCN):
                idx = q * TCN + tch
                if q == 0:
                    if tch >= 2:
                        eng.wait_ge(t3, idx - 1)
                        q3s.wait(eng, mk[("zs", tch - 2)])
                    for p in range(4):
                        pidx = tch * 4 + p
                        eng.wait_ge(s3v, pidx + 1)
                        if pidx >= 2:
                            eng.wait_ge(t3t, 2 * (pidx - 1))
                        eng.tensor_scalar(out=zqp[pidx % 2][:], in0=vv3[pidx % 2][:],
                                          scalar1=c.cr_t[:], scalar2=c.cr_t[:],
                                          op0=OP.add,
                                          op1=OP.subtract).then_inc(v3q, 1)
                        eng.drain()
                        if p >= 1:
                            for g in (2 * (p - 1), 2 * (p - 1) + 1):
                                eng.wait_ge(t3t, 8 * tch + g + 1)
                                eng.tensor_copy(
                                    out=zqf[tch % 2][:, g * 1024:(g + 1) * 1024],
                                    in_=p3t[g % 2][:]).then_inc(v3e, 1)
                    for g in (6, 7):
                        eng.wait_ge(t3t, 8 * tch + g + 1)
                        eng.tensor_copy(
                            out=zqf[tch % 2][:, g * 1024:(g + 1) * 1024],
                            in_=p3t[g % 2][:]).then_inc(v3e, 1)
                eng.wait_ge(s3e, idx + 1)
                q3y.wait(eng, mk[("y", idx)])
                if idx >= 2:
                    qO.wait(eng, mk[("o", idx - 2)])
                eng.tensor_add(out=outb[idx % 2][:], in0=t3sb[idx % 2][:],
                               in1=ybuf[idx % 2][:]).then_inc(v3y, 1)
        eng.memset(c.dum[:, 1:2], 0.0).then_inc(c.phz, 1)

    @blk.tensor
    def _(eng):
        eng.wait_ge(c.phz, PB)
        for q in range(NQ):
            for tch in range(TCN):
                idx = q * TCN + tch
                if q == 0:
                    for g in range(8):
                        pidx = tch * 4 + g // 2
                        eng.wait_ge(v3q, pidx + 1)
                        gidx = 8 * tch + g
                        if gidx >= 2:
                            eng.wait_ge(v3e, gidx - 1)
                        for k in range(8):
                            col = (g % 2) * 1024 + k * 128
                            src = zqp[pidx % 2][:, (g % 2) * 1024 + k * 128:
                                                (g % 2) * 1024 + (k + 1) * 128]
                            ii = eng.transpose(p3t[g % 2][:, k * 128:(k + 1) * 128],
                                               src, c.idb_t[:])
                            if k == 7:
                                ii.then_inc(t3t, 1)
                    eng.wait_ge(v3e, 8 * (tch + 1))
                else:
                    q3q.wait(eng, mk[("zq", idx)])
                if idx >= 2:
                    eng.wait_ge(s3e, idx - 1)
                for icg in range(8):
                    eng.wait_ge(g3w, 64 * q + 8 * (icg + 1))
                    for ic8 in range(8):
                        ic = icg * 8 + ic8
                        lt = zqf[tch % 2][:, ic * 128:(ic + 1) * 128]
                        ii = eng.matmul(p3[idx % 2][:], lt,
                                        wdq[:, ic * 512:(ic + 1) * 512],
                                        start=(ic == 0), stop=(ic == 63))
                        if ic == 63:
                            ii.then_inc(t3, 1)
                        if tch == TCN - 1 and ((ic8 == 7 and ic != 63) or ic == 62):
                            ii.then_inc(t3f, 1)
        eng.wait_ge(s3e, NQ * TCN)
        eng.transpose(p3t[0][:, 0:128], c.idb_t[:], c.idb_t[:]).then_inc(c.phz, 1)

    es.close()


_BUILD_CACHE = {}


def _get_nc(T):
    if T not in _BUILD_CACHE:
        _BUILD_CACHE[T] = build_nc(T)
    return _BUILD_CACHE[T]


def _hd_perm():
    perm = np.zeros(HD, np.int64)
    for b in range(16):
        for r in range(128):
            head = 8 * (b // 4) + (b % 4) + (4 if r >= 64 else 0)
            perm[b * 128 + r] = head * 64 + (r % 64)
    return perm


def _prep_inputs(T, L0_attn_x, L0_attn_Wq, L0_attn_bq, L0_attn_Wk, L0_attn_bk,
                 L0_attn_Wv, L0_attn_bv, L0_attn_Wo, L0_attn_bo, L0_attn_sinks,
                 L0_ffn_w_gate, L0_ffn_w_up, L0_ffn_w_down):
    cst = _host_consts(T)
    perm = _hd_perm()
    shared = dict(
        Wq=np.ascontiguousarray(np.asarray(L0_attn_Wq, np.float32)[:, perm]),
        Wk=np.ascontiguousarray(L0_attn_Wk, np.float32),
        Wv=np.ascontiguousarray(L0_attn_Wv, np.float32),
        Wo=np.ascontiguousarray(np.asarray(L0_attn_Wo, np.float32)[perm, :]),
        bq_r=np.ascontiguousarray(
            np.asarray(L0_attn_bq, np.float32)[perm].reshape(HDCN, 128).T),
        bk_r=np.ascontiguousarray(np.asarray(L0_attn_bk, np.float32).reshape(4, 128).T),
        bv_r=np.ascontiguousarray(np.asarray(L0_attn_bv, np.float32).reshape(4, 128).T),
        bo_row=np.ascontiguousarray(np.asarray(L0_attn_bo, np.float32).reshape(1, E)),
        sinks_b=np.ascontiguousarray(
            np.broadcast_to(np.asarray(L0_attn_sinks, np.float32)[None, :], (128, H))),
        wgT=np.ascontiguousarray(np.asarray(L0_ffn_w_gate, np.float32).T),
        wuT=np.ascontiguousarray(np.asarray(L0_ffn_w_up, np.float32).T),
        wdT=np.ascontiguousarray(np.asarray(L0_ffn_w_down, np.float32).T),
        cos_fm=cst["cos_fm"], sin_fm=cst["sin_fm"], idb=cst["idb"], idf=cst["idf"],
        RT=cst["RT"], ones1=cst["ones1"], mask0=cst["mask0"], mask1=cst["mask1"],
        mask2=cst["mask2"],
    )
    xs = np.ascontiguousarray(np.asarray(L0_attn_x, np.float32))
    in_maps = []
    for b in range(xs.shape[0]):
        m = dict(shared)
        m["x"] = np.ascontiguousarray(xs[b, :T, :])
        in_maps.append(m)
    return in_maps


def kernel(**inputs):
    T = 2048
    nc = _get_nc(T)
    in_maps = _prep_inputs(T, **inputs)
    res = run_bass_kernel_spmd(nc, in_maps, core_ids=list(range(len(in_maps))))
    return np.stack([r["y"] for r in res.results], axis=0)


# version 15 (final)
# speedup vs baseline: 1.0343x; 1.0343x over previous
import numpy as np
import ml_dtypes
import concourse.bass as bass
import concourse.mybir as mybir
from concourse.bass_utils import run_bass_kernel_spmd
from contextlib import ExitStack

F32 = mybir.dt.float32
BF16 = mybir.dt.bfloat16
AX = mybir.AxisListType.X
AF = mybir.ActivationFunctionType
OP = mybir.AluOpType

C23 = 12582912.0
C52 = 6755399441055744.0
NEG = -1.0e30

E = 2048; HD = 2048; KVD = 512; D = 64; H = 32; NKV = 8; GQ = 4; II = 8192
ECN = 16; ICN = 64; HDCN = 16

N_PHASES = 6


def _host_consts(T):
    import jax
    import jax.numpy as jnp
    with jax.default_device(jax.devices("cpu")[0]):
        inv_freq = 1.0 / 10000.0 ** (jnp.arange(0, D, 2, dtype=jnp.float32) / D)
        t_ = jnp.arange(T, dtype=jnp.float32)
        freqs = t_[:, None] * inv_freq[None, :]
        emb = jnp.concatenate([freqs, freqs], axis=-1)
        cos64 = np.asarray(jnp.cos(emb), np.float32).T
        sin64 = np.asarray(jnp.sin(emb), np.float32).T
    cos_fm = np.ascontiguousarray(np.concatenate([cos64, cos64], 0).astype(np.float32))
    sin_fm = np.ascontiguousarray(np.concatenate([sin64, sin64], 0).astype(np.float32))
    idb = np.eye(128, dtype=ml_dtypes.bfloat16)
    idf = np.eye(128, dtype=np.float32)
    R = np.zeros((64, 64), np.float32)
    R[np.arange(32), np.arange(32) + 32] = -1.0
    R[np.arange(32) + 32, np.arange(32)] = 1.0
    RT = np.zeros((128, 128), np.float32)
    RT[:64, :64] = R.T
    RT[64:, 64:] = R.T
    ones1 = np.ones((1, 128), np.float32)
    iq = np.arange(128)[:, None]
    jk = np.arange(128)[None, :]
    m_old = np.where(jk > iq, 0.0, NEG).astype(np.float32)
    m_cur = np.where(jk <= iq, 0.0, NEG).astype(np.float32)
    full = np.zeros((128, 128), np.float32)
    neg = np.full((128, 128), NEG, np.float32)
    mask2 = np.ascontiguousarray(np.concatenate([m_old, full, m_cur], 1).astype(ml_dtypes.bfloat16))
    mask1 = np.ascontiguousarray(np.concatenate([neg, full, m_cur], 1).astype(ml_dtypes.bfloat16))
    mask0 = np.ascontiguousarray(np.concatenate([neg, neg, m_cur], 1).astype(ml_dtypes.bfloat16))
    return dict(cos_fm=cos_fm, sin_fm=sin_fm, idb=np.asarray(idb), idf=idf, RT=RT,
                ones1=ones1, mask0=mask0, mask1=mask1, mask2=mask2)


class Ctx:
    pass


def build_nc(T):
    c = Ctx()
    c.T = T
    c.TCN = T // 128
    c.SCN = T // 512
    nc = bass.Bass("TRN2", target_bir_lowering=False)
    c.nc = nc

    def din(name, shape, dt=F32):
        return nc.dram_tensor(name, shape, dt, kind="ExternalInput").ap()

    def scrd(name, shape, dt):
        return nc.dram_tensor(name, shape, dt, kind="Internal").ap()

    c.x_d = din("x", [T, E])
    c.Wq_d = din("Wq", [E, HD]); c.Wk_d = din("Wk", [E, KVD]); c.Wv_d = din("Wv", [E, KVD])
    c.Wo_d = din("Wo", [HD, E])
    c.bq_d = din("bq_r", [128, HDCN]); c.bk_d = din("bk_r", [128, 4]); c.bv_d = din("bv_r", [128, 4])
    c.bo_d = din("bo_row", [1, E])
    c.sinks_d = din("sinks_b", [128, H])
    c.wgT_d = din("wgT", [E, II]); c.wuT_d = din("wuT", [E, II]); c.wdT_d = din("wdT", [II, E])
    c.cos_d = din("cos_fm", [128, T]); c.sin_d = din("sin_fm", [128, T])
    c.idb_d = din("idb", [128, 128], BF16); c.idf_d = din("idf", [128, 128])
    c.RT_d = din("RT", [128, 128]); c.on1_d = din("ones1", [1, 128])
    c.mask_d = [din("mask0", [128, 384], BF16), din("mask1", [128, 384], BF16),
                din("mask2", [128, 384], BF16)]
    c.y_d = nc.dram_tensor("y", [T, E], F32, kind="ExternalOutput").ap()

    c.wqh_d = scrd("wq_hi", [E, HD], BF16); c.wql_d = scrd("wq_lo", [E, HD], BF16)
    c.wkh_d = scrd("wk_hi", [E, KVD], BF16); c.wkl_d = scrd("wk_lo", [E, KVD], BF16)
    c.wvh_d = scrd("wv_hi", [E, KVD], BF16); c.wvl_d = scrd("wv_lo", [E, KVD], BF16)
    c.woh_d = scrd("wo_hi", [HD, E], BF16); c.wol_d = scrd("wo_lo", [HD, E], BF16)
    c.xnh_d = scrd("xnh_sp", [ECN, 128, T], BF16)
    c.xnl_d = scrd("xnl_sp", [ECN, 128, T], BF16)
    c.xq_d = scrd("xq_sp", [ECN, 128, T], BF16)
    c.oh_d = scrd("oh_sp", [HDCN, 128, T], BF16)
    c.ol_d = scrd("ol_sp", [HDCN, 128, T], BF16)
    c.ysp_d = scrd("y_sp", [T, E], F32)
    c.z_d = scrd("z_sp", [T, II], F32)
    c.zq_d = scrd("zq_sp", [ICN, 128, T], BF16)

    c.ges = ExitStack()

    def gsb(name, shape, dt):
        return c.ges.enter_context(nc.sbuf_tensor(name, shape, dt))

    c.gsb = gsb
    c.sem = lambda name: c.ges.enter_context(nc.semaphore(name))

    class DQ:
        def __init__(self, name, nslots):
            self.sems = [c.sem(f"{name}{i}") for i in range(nslots)]
            self.tot = [0] * nslots
        def inc(self, inst, slot=0):
            inst.then_inc(self.sems[slot], 16)
            self.tot[slot] += 16
            return (slot, self.tot[slot])
        def wait(self, eng, mark):
            eng.wait_ge(self.sems[mark[0]], mark[1])
    c.DQ = DQ

    c.u_t = gsb("u_t", [128, 1], F32)
    c.cr_t = gsb("cr_t", [128, 1], F32)
    c.r127 = gsb("r127", [128, 1], F32)
    c.xs_all = gsb("xs_all", [128, c.TCN], F32)
    c.xswg = gsb("xswg", [128, c.TCN], F32)
    c.xswu = gsb("xswu", [128, c.TCN], F32)
    c.s2w_all = gsb("s2w_all", [128, c.TCN], F32)
    c.s2i_all = gsb("s2i_all", [128, c.TCN], F32)
    c.rmax_z = gsb("rmax_z", [128, c.TCN], F32)
    c.wsum = gsb("wsum", [128, 192], F32)
    c.wsc = gsb("wsc", [128, 8], F32)
    c.idb_t = gsb("idb_t", [128, 128], BF16)
    c.idf_t = gsb("idf_t", [128, 128], F32)
    c.RT_t = gsb("RT_t", [128, 128], F32)
    c.on1_t = gsb("on1_t", [1, 128], F32)
    c.nsink = gsb("nsink", [128, H], F32)
    c.bq_t = gsb("bq_t", [128, HDCN], F32)
    c.bk_t = gsb("bk_t", [128, 4], F32)
    c.bv_t = gsb("bv_t", [128, 4], F32)
    c.dum = gsb("dum", [128, 8], F32)
    c.eps_t = gsb("eps_t", [128, 1], F32)
    c.c125 = gsb("c125", [128, 1], F32)

    c.phz = c.sem("phz")
    c.PH = 0

    c.blk = c.ges.enter_context(nc.Block())
    phases = [(_phaseW, False), (_phase0, True), (_phase1, True),
              (_phase15, True), (_phase2, True), (_phase3, True)]
    for ph, bump in phases[:N_PHASES]:
        ph(c)
        if bump:
            c.PH += 1
    c.ges.close()
    return nc


def _phaseW(c):
    nc, blk = c.nc, c.blk
    es = ExitStack()
    sb = lambda n, s, d: es.enter_context(nc.sbuf_tensor(n, s, d))
    psm = lambda n, s, d=F32: es.enter_context(nc.psum_tensor(n, s, d))

    WJ = []
    for (srcd, hid, lod, rows, cols) in [
        (c.Wq_d, c.wqh_d, c.wql_d, E, HD), (c.Wk_d, c.wkh_d, c.wkl_d, E, KVD),
        (c.Wv_d, c.wvh_d, c.wvl_d, E, KVD), (c.Wo_d, c.woh_d, c.wol_d, HD, E),
    ]:
        for r0 in range(0, rows, 128):
            WJ.append((srcd, hid, lod, r0, cols))

    wbuf = [sb("wbuf0", [128, 2048], F32), sb("wbuf1", [128, 2048], F32)]
    whib = [sb("whib0", [128, 2048], BF16), sb("whib1", [128, 2048], BF16)]
    wlob = [sb("wlob0", [128, 2048], BF16), sb("wlob1", [128, 2048], BF16)]

    qC = c.DQ("qC", 1)
    qWl = c.DQ("qWl", 2)
    qWs = c.DQ("qWs", 2)
    vWs = c.sem("vWs")
    mk = {}

    @blk.scalar
    def _(eng):
        for j, (srcd, hid, lod, r0, cols) in enumerate(WJ):
            if j >= 2:
                eng.wait_ge(vWs, j - 1)
            mk[("l", j)] = qWl.inc(eng.dma_start(out=wbuf[j % 2][:, 0:cols],
                                                 in_=srcd[r0:r0 + 128, :]), j % 2)
        pass

    @blk.sync
    def _(eng):
        for dst, src in [(c.idb_t, c.idb_d), (c.idf_t, c.idf_d), (c.RT_t, c.RT_d),
                         (c.on1_t, c.on1_d), (c.nsink, c.sinks_d), (c.bq_t, c.bq_d),
                         (c.bk_t, c.bk_d), (c.bv_t, c.bv_d)]:
            mk["consts"] = qC.inc(eng.dma_start(out=dst[:], in_=src), 0)
        for j, (srcd, hid, lod, r0, cols) in enumerate(WJ):
            eng.wait_ge(vWs, j + 1)
            qWs.inc(eng.dma_start(out=hid[r0:r0 + 128, :],
                                  in_=whib[j % 2][:, 0:cols]), j % 2)
            mk[("st", j)] = qWs.inc(eng.dma_start(out=lod[r0:r0 + 128, :],
                                                  in_=wlob[j % 2][:, 0:cols]), j % 2)

    @blk.vector
    def _(eng):
        eng.memset(c.u_t[:], 0.75)
        eng.memset(c.r127[:], 127.0)
        eng.memset(c.rmax_z[:], 0.0)
        eng.memset(c.eps_t[:], 1e-5)
        eng.memset(c.c125[:], 0.125)
        eng.drain()
        eng.tensor_scalar(out=c.u_t[:], in0=c.u_t[:], scalar1=C52, scalar2=C52,
                          op0=OP.add, op1=OP.subtract)
        eng.drain()
        eng.tensor_scalar(out=c.cr_t[:], in0=c.u_t[:], scalar1=C52, scalar2=C23,
                          op0=OP.mult, op1=OP.add)
        eng.reciprocal(out=c.r127[:], in_=c.r127[:])
        qC.wait(eng, mk["consts"])
        eng.tensor_scalar(out=c.nsink[:], in0=c.nsink[:], scalar1=-1.0,
                          scalar2=None, op0=OP.mult)
        for j, (srcd, hid, lod, r0, cols) in enumerate(WJ):
            qWl.wait(eng, mk[("l", j)])
            if j >= 2:
                qWs.wait(eng, mk[("st", j - 2)])
            eng.tensor_copy(out=whib[j % 2][:, 0:cols], in_=wbuf[j % 2][:, 0:cols])
            eng.drain()
            eng.tensor_sub(out=wlob[j % 2][:, 0:cols], in0=wbuf[j % 2][:, 0:cols],
                           in1=whib[j % 2][:, 0:cols]).then_inc(vWs, 1)
        qWs.wait(eng, mk[("st", len(WJ) - 2)])
        qWs.wait(eng, mk[("st", len(WJ) - 1)])

    c.qC = qC
    c.mkC = mk["consts"]
    c._w_es = es


def _phase0(c):
    nc, blk, T, TCN = c.nc, c.blk, c.T, c.TCN
    es = ExitStack()
    sb = lambda n, s, d: es.enter_context(nc.sbuf_tensor(n, s, d))
    psm = lambda n, s, d=F32: es.enter_context(nc.psum_tensor(n, s, d))

    xb = [sb("xb0", [128, E], F32), sb("xb1", [128, E], F32)]
    xnt = sb("xnt", [128, E], F32)
    vqt = sb("vqt", [128, E], F32)
    m23t = sb("m23t", [128, E], F32)
    ddt = sb("ddt", [128, E], F32)
    xqt = sb("xqt", [128, E], BF16)
    xnhb = sb("xnhb", [128, E], BF16)
    xnlb = sb("xnlb", [128, E], BF16)
    sq_scr = sb("sq_scr", [128, E], F32)
    stt = sb("stt", [128, 8], F32)
    stg = [sb(f"stg{i}", [128, E], BF16) for i in range(6)]
    p0ps = [psm("p0ps0", [128, 1024], BF16), psm("p0ps1", [128, 1024], BF16)]

    qX = c.DQ("qX", 2); qP = c.DQ("qP", 2)
    v0 = c.sem("v0"); s0 = c.sem("s0"); g0 = c.sem("g0"); t0 = c.sem("t0")
    mk = {}
    NG = 6
    PB = 4 * c.PH

    def v0_var(tc): return 11 * tc + 1
    def v0_xdone(tc): return 11 * tc + 2
    def v0_v(tc): return 11 * tc + 3
    def v0_split(tc): return 11 * tc + 4
    def v0_xq(tc): return 11 * tc + 5
    def v0_ev(tc, g): return 11 * tc + 6 + g
    def v0_stage(tc): return 11 * (tc + 1)

    @blk.sync
    def _(eng):
        eng.wait_ge(c.phz, PB)
        for tc in range(TCN + 1):
            if tc < TCN:
                if tc >= 2:
                    eng.wait_ge(v0, v0_xdone(tc - 2))
                    eng.wait_ge(s0, 2 * (tc - 2) + 1)
                mk[("x", tc)] = qX.inc(eng.dma_start(
                    out=xb[tc % 2][:],
                    in_=c.x_d[tc * 128:(tc + 1) * 128, :]), tc % 2)
            tcs = tc - 1
            if 0 <= tcs < TCN:
                eng.wait_ge(v0, v0_stage(tcs))
                pb = (tcs % 2) * 3
                for i, dd_ in enumerate([c.xnh_d, c.xnl_d, c.xq_d]):
                    sp3 = stg[pb + i][:].rearrange("p (ec t) -> p ec t", ec=ECN)
                    mk[("sp", tcs)] = qP.inc(eng.dma_start(
                        out=dd_.transpose([1, 0, 2])[:, :, tcs * 128:(tcs + 1) * 128],
                        in_=sp3), tcs % 2)

    @blk.scalar
    def _(eng):
        eng.wait_ge(c.phz, PB)
        for tc in range(TCN):
            qX.wait(eng, mk[("x", tc)])
            eng.activation(out=sq_scr[:], in_=xb[tc % 2][:], func=AF.Square,
                           bias=0.0, scale=1.0,
                           accum_out=stt[:, 2:3]).then_inc(s0, 1)
            eng.wait_ge(v0, v0_var(tc))
            eng.activation(out=stt[:, 4:5], in_=stt[:, 3:4], func=AF.Sqrt,
                           bias=c.eps_t[:], scale=1.0).then_inc(s0, 1)
        eng.activation(out=c.dum[:, 0:1], in_=c.u_t[:], func=AF.Copy, bias=0.0,
                       scale=0.0).then_inc(c.phz, 1)

    @blk.gpsimd
    def _(eng):
        eng.wait_ge(c.phz, PB)
        eng.memset(c.dum[:, 2:3], 0.0).then_inc(c.phz, 1)

    @blk.vector
    def _(eng):
        eng.wait_ge(c.phz, PB)
        for tc in range(TCN):
            qX.wait(eng, mk[("x", tc)])
            if tc >= 1:
                eng.wait_ge(t0, NG * tc)
            x_t = xb[tc % 2]
            eng.tensor_reduce(out=stt[:, 0:1], in_=x_t[:], axis=AX, op=OP.add)
            eng.drain()
            eng.tensor_scalar(out=stt[:, 1:2], in0=stt[:, 0:1],
                              scalar1=1.0 / 2048.0, scalar2=None, op0=OP.mult)
            eng.wait_ge(s0, 2 * tc + 1)
            eng.drain()
            eng.tensor_scalar(out=stt[:, 3:4], in0=stt[:, 2:3],
                              scalar1=1.0 / 2048.0, scalar2=None, op0=OP.mult)
            eng.tensor_mul(out=stt[:, 5:6], in0=stt[:, 1:2], in1=stt[:, 1:2])
            eng.drain()
            eng.tensor_sub(out=stt[:, 3:4], in0=stt[:, 3:4],
                           in1=stt[:, 5:6]).then_inc(v0, 1)
            eng.wait_ge(s0, 2 * tc + 2)
            eng.reciprocal(out=stt[:, 5:6], in_=stt[:, 4:5])
            eng.drain()
            eng.tensor_scalar(out=xnt[:], in0=x_t[:], scalar1=stt[:, 1:2],
                              scalar2=stt[:, 5:6], op0=OP.subtract, op1=OP.mult)
            eng.drain()
            eng.tensor_copy(out=xnhb[:], in_=xnt[:]).then_inc(v0, 1)
            eng.tensor_reduce(out=stt[:, 6:7], in_=xnt[:], axis=AX, op=OP.max,
                              apply_absolute_value=True)
            eng.drain()
            eng.tensor_scalar(out=stt[:, 6:7], in0=stt[:, 6:7], scalar1=c.r127[:],
                              scalar2=1e-5, op0=OP.mult, op1=OP.add)
            eng.drain()
            eng.tensor_copy(out=c.xs_all[:, tc:tc + 1], in_=stt[:, 6:7])
            eng.reciprocal(out=stt[:, 7:8], in_=stt[:, 6:7])
            eng.drain()
            eng.tensor_scalar(out=vqt[:], in0=xnt[:], scalar1=stt[:, 7:8],
                              scalar2=None, op0=OP.mult)
            eng.drain()
            eng.tensor_scalar(out=m23t[:], in0=vqt[:], scalar1=C23, scalar2=C23,
                              op0=OP.add, op1=OP.subtract)
            eng.tensor_scalar(out=ddt[:], in0=vqt[:], scalar1=C52, scalar2=C52,
                              op0=OP.add, op1=OP.subtract).then_inc(v0, 1)
            eng.drain()
            eng.tensor_sub(out=ddt[:], in0=ddt[:], in1=m23t[:])
            eng.tensor_sub(out=xnlb[:], in0=xnt[:], in1=xnhb[:]).then_inc(v0, 1)
            eng.drain()
            eng.scalar_tensor_tensor(out=xqt[:], in0=ddt[:], scalar=c.u_t[:],
                                     in1=m23t[:], op0=OP.mult,
                                     op1=OP.add).then_inc(v0, 1)
            if tc >= 2:
                qP.wait(eng, mk[("sp", tc - 2)])
            pb = (tc % 2) * 3
            for g in range(NG):
                eng.wait_ge(t0, NG * tc + g + 1)
                eng.tensor_copy(
                    out=stg[pb + g // 2][:, (g % 2) * 1024:(g % 2) * 1024 + 1024],
                    in_=p0ps[g % 2][:]).then_inc(v0, 1)
        qP.wait(eng, mk[("sp", TCN - 2)])
        qP.wait(eng, mk[("sp", TCN - 1)])
        qX.wait(eng, mk[("x", TCN - 2)])
        qX.wait(eng, mk[("x", TCN - 1)])
        eng.memset(c.dum[:, 1:2], 0.0).then_inc(c.phz, 1)

    @blk.tensor
    def _(eng):
        eng.wait_ge(c.phz, PB)
        c.qC.wait(eng, c.mkC)
        for tc in range(TCN):
            srcs = [xnhb, xnhb, xnlb, xnlb, xqt, xqt]
            for g in range(NG):
                eng.wait_ge(v0, v0_split(tc) if g < 4 else v0_xq(tc))
                gi = tc * NG + g
                if gi >= 2:
                    eng.wait_ge(v0, v0_ev((gi - 2) // NG, (gi - 2) % NG))
                cb = (g % 2) * 1024
                for k in range(8):
                    i = eng.transpose(p0ps[g % 2][:, k * 128:(k + 1) * 128],
                                      srcs[g][:, cb + k * 128:cb + (k + 1) * 128],
                                      c.idb_t[:])
                    if k == 7:
                        i.then_inc(t0, 1)
        eng.wait_ge(v0, v0_ev(TCN - 1, NG - 1))
        eng.transpose(p0ps[0][:, 0:128], xqt[:, 0:128],
                      c.idb_t[:]).then_inc(c.phz, 1)

    es.close()
    c._w_es.close()


def _phase1(c):
    nc, blk, T, TCN, SCN = c.nc, c.blk, c.T, c.TCN, c.SCN
    es = ExitStack()
    sb = lambda n, s, d: es.enter_context(nc.sbuf_tensor(n, s, d))
    psm = lambda n, s, d=F32: es.enter_context(nc.psum_tensor(n, s, d))
    PB = 4 * c.PH

    cos_t = sb("cos_t", [128, T], F32)
    sin_t = sb("sin_t", [128, T], F32)
    mask_t = sb("mask_t", [128, 3 * 384], BF16)
    xnh_s = sb("xnh_s", [128, 16 * 512], BF16)
    xnl_s = sb("xnl_s", [128, 16 * 512], BF16)
    wp_h = [sb("wp_h0", [128, 768], BF16), sb("wp_h1", [128, 768], BF16)]
    wp_l = [sb("wp_l0", [128, 768], BF16), sb("wp_l1", [128, 768], BF16)]
    qsb = [sb("qsb0", [128, 512], F32), sb("qsb1", [128, 512], F32)]
    rt1 = sb("rt1", [128, 512], F32)
    rt2 = sb("rt2", [128, 512], F32)
    qr_h = sb("qr_h", [128, 16 * 512], BF16)
    qr_l = sb("qr_l", [128, 16 * 512], BF16)
    kr_h = sb("kr_h", [128, 4 * 768], BF16)
    kr_l = sb("kr_l", [128, 4 * 768], BF16)
    vr_h = sb("vr_h", [128, 6 * 512], BF16)
    vr_l = sb("vr_l", [128, 6 * 512], BF16)
    p_sb2 = [sb("p_sb0", [128, 4 * 384], F32), sb("p_sb1", [128, 4 * 384], F32)]
    smt2 = [sb("smt0", [128, 24], F32), sb("smt1", [128, 24], F32)]
    pt_h2 = [sb("pt_h0", [128, 3 * 512], BF16), sb("pt_h1", [128, 3 * 512], BF16)]
    pt_l2 = [sb("pt_l0", [128, 3 * 512], BF16), sb("pt_l1", [128, 3 * 512], BF16)]
    ost_h = [sb("ost_h0", [128, 2048], BF16), sb("ost_h1", [128, 2048], BF16)]
    ost_l = [sb("ost_l0", [128, 2048], BF16), sb("ost_l1", [128, 2048], BF16)]

    p1q = [psm("p1qA", [128, 2048], F32), psm("p1qB", [128, 2048], F32)]

    def acc_ap(oi):
        if oi < 4:
            return p1q[0][:, oi * 512:(oi + 1) * 512]
        return p1q[1][:, (oi - 4) * 512:(oi - 3) * 512]

    q1n = c.DQ("q1n", 1)
    q1m = c.DQ("q1m", 1)
    q1w = c.DQ("q1w", 2)
    q1o = c.DQ("q1o", 2)
    t1a = c.sem("t1a"); t1mm = c.sem("t1mm"); t1r = c.sem("t1r")
    s1e = c.sem("s1e"); v1r = c.sem("v1r")
    t1s = c.sem("t1s"); v1m = c.sem("v1m"); s1x = c.sem("s1x")
    v1d = c.sem("v1d"); t1p = c.sem("t1p")
    s1h = c.sem("s1h"); v1l = c.sem("v1l"); t1v = c.sem("t1v")
    s1o = c.sem("s1o"); v1o = c.sem("v1o")
    s1n = c.sem("s1n")
    mk = {}

    OCS = ([("q", i) for i in range(16)] + [("k", i) for i in range(4)]
           + [("v", i) for i in range(4)])
    GRP = [OCS[0:6], OCS[6:12], OCS[12:18], OCS[18:24]]

    def srcs_of(kind):
        return {"q": (c.wqh_d, c.wql_d), "k": (c.wkh_d, c.wkl_d),
                "v": (c.wvh_d, c.wvl_d)}[kind]

    def grp_runs2(gi):
        out = []
        off = 0
        k = 0
        members = GRP[gi]
        while k < len(members):
            kind, i = members[k]
            hd_, ld_ = srcs_of(kind)
            w = 128
            k2 = k + 1
            while k2 < len(members):
                kind2, i2 = members[k2]
                hd2, _ = srcs_of(kind2)
                if hd2 is hd_ and i2 * 128 == i * 128 + w:
                    w += 128
                    k2 += 1
                else:
                    break
            out.append((hd_, ld_, i * 128, w, off))
            off += w
            k = k2
        return out

    NW = 8 * TCN

    @blk.sync
    def _(eng):
        eng.wait_ge(c.phz, PB)
        mk["cs"] = q1m.inc(eng.dma_start(out=cos_t[:], in_=c.cos_d), 0)
        mk["cs"] = q1m.inc(eng.dma_start(out=sin_t[:], in_=c.sin_d), 0)
        for i in range(3):
            mk["cs"] = q1m.inc(eng.dma_start(out=mask_t[:, i * 384:(i + 1) * 384],
                                             in_=c.mask_d[i]), 0)
        for sc in range(SCN):
            if sc >= 1:
                eng.wait_ge(t1a, 24 * sc)
            for ec in range(16):
                q1n.inc(eng.dma_start(out=xnh_s[:, ec * 512:(ec + 1) * 512],
                                      in_=c.xnh_d[ec, :, sc * 512:(sc + 1) * 512]), 0)
                mk[("xn", sc)] = q1n.inc(
                    eng.dma_start(out=xnl_s[:, ec * 512:(ec + 1) * 512],
                                  in_=c.xnl_d[ec, :, sc * 512:(sc + 1) * 512]), 0)
            for j in range(4):
                tch = sc * 4 + j
                eng.wait_ge(s1o, 8 * (tch + 1))
                eng.wait_ge(v1o, 8 * (tch + 1))
                ob = tch % 2
                sp3h = ost_h[ob][:].rearrange("p (hc t) -> p hc t", hc=16)
                sp3l = ost_l[ob][:].rearrange("p (hc t) -> p hc t", hc=16)
                q1o.inc(eng.dma_start(
                    out=c.oh_d.transpose([1, 0, 2])[:, :, tch * 128:(tch + 1) * 128],
                    in_=sp3h), ob)
                mk[("osp", tch)] = q1o.inc(eng.dma_start(
                    out=c.ol_d.transpose([1, 0, 2])[:, :, tch * 128:(tch + 1) * 128],
                    in_=sp3l), ob)

    @blk.gpsimd
    def _(eng):
        eng.wait_ge(c.phz, PB)
        eng.memset(c.dum[:, 2:3], 0.0).then_inc(c.phz, 1)

    def ocn(sc, gi, oi):
        return sc * 24 + gi * 6 + oi + 1

    @blk.scalar
    def _(eng):
        eng.wait_ge(c.phz, PB)
        for sc in range(SCN):
            for gi in range(4):
                runs = grp_runs2(gi)
                for ec in range(16):
                    gec = sc * 64 + gi * 16 + ec
                    if gec >= 2:
                        gp = gec - 2
                        if gp % 16 == 15:
                            eng.wait_ge(t1a, ((gp // 16) + 1) * 6)
                        else:
                            eng.wait_ge(t1mm, gp + 1 - (gp // 16))
                    for (hd_, ld_, c0, w, boff) in runs:
                        q1w.inc(eng.dma_start(
                            out=wp_h[gec % 2][:, boff:boff + w],
                            in_=hd_[ec * 128:(ec + 1) * 128, c0:c0 + w]), gec % 2)
                        mk[("wp", gec)] = q1w.inc(eng.dma_start(
                            out=wp_l[gec % 2][:, boff:boff + w],
                            in_=ld_[ec * 128:(ec + 1) * 128, c0:c0 + w]), gec % 2)
            def s_exp(wi):
                wp = wi % 2
                p_sb = p_sb2[wp]; smt = smt2[wp]
                eng.wait_ge(v1m, wi + 1)
                if wi >= 2:
                    eng.wait_ge(t1p, wi - 1)
                for h4 in range(4):
                    eng.activation(out=p_sb[:, h4 * 384:(h4 + 1) * 384],
                                   in_=p1q[wp][:, h4 * 512:h4 * 512 + 384],
                                   func=AF.Exp, bias=smt[:, 4 + h4:5 + h4],
                                   scale=c.c125[:],
                                   accum_out=smt[:, 8 + h4:9 + h4])
                eng.drain()
                eng.activation(out=smt[:, 12:16], in_=smt[:, 12:16],
                               func=AF.Exp, bias=0.0,
                               scale=1.0).then_inc(s1x, 1)

            def s_pnorm(wi):
                wp = wi % 2
                p_sb = p_sb2[wp]; smt = smt2[wp]
                eng.wait_ge(v1d, wi + 1)
                for h4 in range(4):
                    ii = eng.activation(out=p_sb[:, h4 * 384:(h4 + 1) * 384],
                                        in_=p_sb[:, h4 * 384:(h4 + 1) * 384],
                                        func=AF.Copy, bias=0.0,
                                        scale=smt[:, 20 + h4:21 + h4])
                    if h4 == 3:
                        ii.then_inc(s1n, 1)

            def s_pth(wi):
                wp = wi % 2
                pt_h = pt_h2[wp]
                eng.wait_ge(t1p, wi + 1)
                for kc in range(3):
                    ii = eng.activation(out=pt_h[:, kc * 512:(kc + 1) * 512],
                                        in_=p1q[wp][:, (kc + 1) * 512:(kc + 2) * 512],
                                        func=AF.Copy,
                                        bias=0.0, scale=1.0)
                    if kc == 2:
                        ii.then_inc(s1h, 1)

            def s_ohi(wi):
                wp = wi % 2
                tch, w = wi // 8, wi % 8
                eng.wait_ge(t1v, wi + 1)
                if tch >= 2 and w == 0:
                    q1o.wait(eng, mk[("osp", tch - 2)])
                orow = (w % 2) * 64
                u_ = w // 2
                eng.activation(out=ost_h[tch % 2][orow:orow + 64, u_ * 512:(u_ + 1) * 512],
                               in_=p1q[wp][orow:orow + 64, 0:512], func=AF.Copy,
                               bias=0.0, scale=1.0).then_inc(s1o, 1)

            s_exp(sc * 32)
            for wv in range(sc * 32, sc * 32 + 32):
                if wv - 1 >= sc * 32:
                    s_ohi(wv - 1)
                s_pnorm(wv)
                s_pth(wv)
                if wv + 1 < sc * 32 + 32:
                    s_exp(wv + 1)
            s_ohi(sc * 32 + 31)
        eng.activation(out=c.dum[:, 0:1], in_=c.u_t[:], func=AF.Copy, bias=0.0,
                       scale=0.0).then_inc(c.phz, 1)

    @blk.vector
    def _(eng):
        eng.wait_ge(c.phz, PB)
        eng.memset(kr_h[:], 0.0)
        eng.memset(kr_l[:], 0.0)
        eng.memset(vr_h[:], 0.0)
        eng.memset(vr_l[:], 0.0)
        for sc in range(SCN):
            cosl = cos_t[:, sc * 512:(sc + 1) * 512]
            sinl = sin_t[:, sc * 512:(sc + 1) * 512]
            if sc >= 1:
                eng.wait_ge(t1s, 32 * sc)
                kh3s = kr_h[:].rearrange("p (k x) -> p k x", k=4)[:, :, 512:768]
                kh3d = kr_h[:].rearrange("p (k x) -> p k x", k=4)[:, :, 0:256]
                kl3s = kr_l[:].rearrange("p (k x) -> p k x", k=4)[:, :, 512:768]
                kl3d = kr_l[:].rearrange("p (k x) -> p k x", k=4)[:, :, 0:256]
                eng.drain()
                eng.tensor_copy(out=kh3d, in_=kh3s)
                eng.tensor_copy(out=kl3d, in_=kl3s)
                eng.drain()
            for gi in range(4):
                for oi, (kind, idx) in enumerate(GRP[gi]):
                    n = ocn(sc, gi, oi)
                    eng.wait_ge(t1a, n)
                    if n >= 3:
                        eng.wait_ge(t1r, n - 2)
                    bias = (c.bq_t[:, idx:idx + 1] if kind == "q" else
                            (c.bk_t[:, idx:idx + 1] if kind == "k"
                             else c.bv_t[:, idx:idx + 1]))
                    eng.tensor_scalar(out=qsb[(n - 1) % 2][:], in0=acc_ap(oi),
                                      scalar1=bias, scalar2=None,
                                      op0=OP.add).then_inc(s1e, 1)
                    eng.wait_ge(t1r, n)
                    if sc >= 1:
                        eng.wait_ge(t1s, 32 * sc)
                        eng.wait_ge(t1v, 32 * sc)
                    if kind in ("q", "k"):
                        eng.drain()
                        eng.tensor_mul(out=rt1[:], in0=qsb[(n - 1) % 2][:], in1=cosl)
                        eng.tensor_mul(out=rt2[:], in0=acc_ap(oi), in1=sinl)
                        eng.drain()
                        eng.tensor_add(out=rt1[:], in0=rt1[:], in1=rt2[:])
                        eng.drain()
                        if kind == "q":
                            dst_h = qr_h[:, idx * 512:(idx + 1) * 512]
                            dst_l = qr_l[:, idx * 512:(idx + 1) * 512]
                            eng.tensor_copy(out=dst_h, in_=rt1[:])
                            eng.drain()
                            eng.tensor_sub(out=dst_l, in0=rt1[:],
                                           in1=dst_h).then_inc(v1r, 1)
                        else:
                            for jj in range(4):
                                slot = 2 + jj
                                dh = kr_h[:, idx * 768 + slot * 128:idx * 768 + slot * 128 + 128]
                                dl = kr_l[:, idx * 768 + slot * 128:idx * 768 + slot * 128 + 128]
                                eng.tensor_copy(out=dh, in_=rt1[:, jj * 128:(jj + 1) * 128])
                                eng.drain()
                                ii = eng.tensor_sub(
                                    out=dl, in0=rt1[:, jj * 128:(jj + 1) * 128], in1=dh)
                                if jj == 3:
                                    ii.then_inc(v1r, 1)
                    else:
                        for jj in range(4):
                            slot = (sc * 4 + jj) % 6
                            src = acc_ap(oi)[:, jj * 128:(jj + 1) * 128]
                            dh = vr_h[:, slot * 512 + idx * 128:slot * 512 + idx * 128 + 128]
                            dl = vr_l[:, slot * 512 + idx * 128:slot * 512 + idx * 128 + 128]
                            eng.tensor_copy(out=dh, in_=src)
                            eng.drain()
                            ii = eng.tensor_sub(out=dl, in0=src, in1=dh)
                            if jj == 3:
                                ii.then_inc(v1r, 1)
            def v_rownm(wi):
                wp = wi % 2
                smt = smt2[wp]
                w = wi % 8
                eng.wait_ge(t1s, wi + 1)
                if wi >= 2:
                    eng.wait_ge(s1x, wi - 1)
                sc3 = p1q[wp][:].rearrange("p (g x) -> p g x", g=4)
                eng.tensor_reduce(out=smt[:, 0:4], in_=sc3[:, :, 0:384],
                                  axis=AX, op=OP.max)
                eng.drain()
                eng.tensor_scalar(out=smt[:, 4:8], in0=smt[:, 0:4],
                                  scalar1=-0.125, scalar2=None, op0=OP.mult)
                eng.drain()
                eng.tensor_tensor(out=smt[:, 4:8], in0=smt[:, 4:8],
                                  in1=c.nsink[:, w * 4:w * 4 + 4],
                                  op=OP.min)
                eng.drain()
                eng.tensor_sub(out=smt[:, 12:16], in0=smt[:, 4:8],
                               in1=c.nsink[:, w * 4:w * 4 + 4]).then_inc(v1m, 1)

            def v_denom(wi):
                wp = wi % 2
                smt = smt2[wp]
                eng.wait_ge(s1x, wi + 1)
                if wi >= 2:
                    eng.wait_ge(s1n, wi - 1)
                eng.tensor_add(out=smt[:, 16:20], in0=smt[:, 8:12],
                               in1=smt[:, 12:16])
                eng.drain()
                eng.reciprocal(out=smt[:, 20:24],
                               in_=smt[:, 16:20]).then_inc(v1d, 1)

            def v_ptl(wi):
                wp = wi % 2
                pt_h = pt_h2[wp]; pt_l = pt_l2[wp]
                eng.wait_ge(t1p, wi + 1)
                eng.wait_ge(s1h, wi + 1)
                for kc in range(3):
                    ii = eng.tensor_sub(out=pt_l[:, kc * 512:(kc + 1) * 512],
                                        in0=p1q[wp][:, (kc + 1) * 512:(kc + 2) * 512],
                                        in1=pt_h[:, kc * 512:(kc + 1) * 512])
                    if kc == 2:
                        ii.then_inc(v1l, 1)

            def v_olo(wi):
                wp = wi % 2
                tch, w = wi // 8, wi % 8
                eng.wait_ge(t1v, wi + 1)
                eng.wait_ge(s1o, wi + 1)
                if tch >= 2 and w == 0:
                    q1o.wait(eng, mk[("osp", tch - 2)])
                orow = (w % 2) * 64
                u_ = w // 2
                eng.tensor_sub(out=ost_l[tch % 2][orow:orow + 64, u_ * 512:(u_ + 1) * 512],
                               in0=p1q[wp][orow:orow + 64, 0:512],
                               in1=ost_h[tch % 2][orow:orow + 64, u_ * 512:(u_ + 1) * 512]
                               ).then_inc(v1o, 1)

            v_rownm(sc * 32)
            v_denom(sc * 32)
            v_rownm(sc * 32 + 1)
            for wv in range(sc * 32, sc * 32 + 32):
                v_ptl(wv)
                if wv + 1 < sc * 32 + 32:
                    v_denom(wv + 1)
                v_olo(wv)
                if wv + 2 < sc * 32 + 32:
                    v_rownm(wv + 2)
        q1o.wait(eng, mk[("osp", TCN - 2)])
        q1o.wait(eng, mk[("osp", TCN - 1)])
        q1n.wait(eng, mk[("xn", SCN - 1)])
        q1w.wait(eng, mk[("wp", SCN * 64 - 2)])
        q1w.wait(eng, mk[("wp", SCN * 64 - 1)])
        eng.memset(c.dum[:, 1:2], 0.0).then_inc(c.phz, 1)

    @blk.tensor
    def _(eng):
        eng.wait_ge(c.phz, PB)
        q1m.wait(eng, mk["cs"])
        for sc in range(SCN):
            for gi in range(4):
                for ec in range(16):
                    gec = sc * 64 + gi * 16 + ec
                    q1w.wait(eng, mk[("wp", gec)])
                    if ec == 0:
                        q1n.wait(eng, mk[("xn", sc)])
                        for oi in range(6):
                            if sc == 0 and gi == 0:
                                break
                            if gi >= 1:
                                n_prev = ocn(sc, gi - 1, oi)
                            else:
                                n_prev = ocn(sc - 1, 3, oi)
                            eng.wait_ge(s1e, n_prev)
                            eng.wait_ge(v1r, n_prev)
                        if sc >= 1 and gi == 0:
                            eng.wait_ge(s1x, 32 * sc)
                            eng.wait_ge(v1m, 32 * sc)
                            eng.wait_ge(s1h, 32 * sc)
                            eng.wait_ge(v1l, 32 * sc)
                            eng.wait_ge(s1o, 32 * sc)
                            eng.wait_ge(v1o, 32 * sc)
                    xh = xnh_s[:, ec * 512:(ec + 1) * 512]
                    xl = xnl_s[:, ec * 512:(ec + 1) * 512]
                    for oi in range(6):
                        lh = wp_h[gec % 2][:, oi * 128:(oi + 1) * 128]
                        ll = wp_l[gec % 2][:, oi * 128:(oi + 1) * 128]
                        st = (ec == 0)
                        sp = (ec == 15)
                        a = acc_ap(oi)
                        eng.matmul(a, lh, xh, start=st, stop=False)
                        eng.matmul(a, lh, xl, start=False, stop=False)
                        ii = eng.matmul(a, ll, xh, start=False, stop=sp)
                        if sp:
                            ii.then_inc(t1a, 1)
                        elif oi == 5:
                            ii.then_inc(t1mm, 1)
                for oi, (kind, idx) in enumerate(GRP[gi]):
                    n = ocn(sc, gi, oi)
                    eng.wait_ge(s1e, n)
                    if kind in ("q", "k"):
                        eng.matmul(acc_ap(oi), c.RT_t[:], qsb[(n - 1) % 2][:],
                                   start=True, stop=True).then_inc(t1r, 1)
                    else:
                        a = acc_ap(oi)
                        for jj in range(4):
                            ii = eng.transpose(
                                a[:, jj * 128:(jj + 1) * 128],
                                qsb[(n - 1) % 2][:, jj * 128:(jj + 1) * 128],
                                c.idf_t[:])
                            if jj == 3:
                                ii.then_inc(t1r, 1)
            def emit_scores(tch, w):
                tca = tch
                wi = tch * 8 + w
                msel = 0 if tca == 0 else (1 if tca == 1 else 2)
                gkv = w
                kvc = gkv // 2
                prow = (gkv % 2) * 64
                if wi == sc * 32:
                    eng.wait_ge(v1r, 24 * (sc + 1))
                if wi >= 2:
                    eng.wait_ge(s1h, wi - 1)
                    eng.wait_ge(v1l, wi - 1)
                    eng.wait_ge(s1o, wi - 1)
                    eng.wait_ge(v1o, wi - 1)
                for h4 in range(4):
                    blkq = 4 * (w // 2) + h4
                    qrow = (w % 2) * 64
                    qcol = blkq * 512 + (tca % 4) * 128
                    kco = kvc * 768 + (tca % 4) * 128
                    qh = qr_h[qrow:qrow + 64, qcol:qcol + 128]
                    ql = qr_l[qrow:qrow + 64, qcol:qcol + 128]
                    kh = kr_h[prow:prow + 64, kco:kco + 384]
                    kl = kr_l[prow:prow + 64, kco:kco + 384]
                    o_ = p1q[wi % 2][:, h4 * 512:h4 * 512 + 384]
                    eng.matmul(o_, c.idb_t[:],
                               mask_t[:, msel * 384:(msel + 1) * 384],
                               start=True, stop=False)
                    eng.matmul(o_, qh, kh, start=False, stop=False)
                    eng.matmul(o_, qh, kl, start=False, stop=False)
                    ii = eng.matmul(o_, ql, kh, start=False, stop=True)
                    if h4 == 3:
                        ii.then_inc(t1s, 1)

            def emit_pt(tch, w):
                wi = tch * 8 + w
                wp = wi % 2
                p_sb = p_sb2[wp]
                eng.wait_ge(s1n, wi + 1)
                for h4 in range(4):
                    for kc in range(3):
                        ii = eng.transpose(
                            p1q[wp][:, (kc + 1) * 512 + h4 * 128:
                                    (kc + 1) * 512 + (h4 + 1) * 128],
                            p_sb[:, h4 * 384 + kc * 128:h4 * 384 + (kc + 1) * 128],
                            c.idf_t[:])
                        if h4 == 3 and kc == 2:
                            ii.then_inc(t1p, 1)

            def emit_av(tch, w):
                wi = tch * 8 + w
                tca = tch
                gkv = w
                wp = wi % 2
                pt_h = pt_h2[wp]; pt_l = pt_l2[wp]
                eng.wait_ge(s1h, wi + 1)
                if wi >= 1:
                    eng.wait_ge(s1o, wi - 1)
                    eng.wait_ge(v1o, wi - 1)
                orow = (w % 2) * 64
                o_ = p1q[wp][orow:orow + 64, 0:512]
                for kc in range(3):
                    slot = (tca - 2 + kc) % 6
                    vco = slot * 512 + gkv * 64
                    vh = vr_h[:, vco:vco + 64]
                    vl = vr_l[:, vco:vco + 64]
                    ph_ = pt_h[:, kc * 512:(kc + 1) * 512]
                    eng.matmul(o_, vh, ph_, start=(kc == 0), stop=False)
                    eng.matmul(o_, vl, ph_, start=False, stop=False)
                eng.wait_ge(v1l, wi + 1)
                for kc in range(3):
                    slot = (tca - 2 + kc) % 6
                    vco = slot * 512 + gkv * 64
                    vh = vr_h[:, vco:vco + 64]
                    pl_ = pt_l[:, kc * 512:(kc + 1) * 512]
                    ii = eng.matmul(o_, vh, pl_, start=False, stop=(kc == 2))
                    if kc == 2:
                        ii.then_inc(t1v, 1)

            waves_sc = [(sc * 4 + j, w) for j in range(4) for w in range(8)]
            emit_scores(*waves_sc[0])
            for iw, (tch_, w_) in enumerate(waves_sc):
                if iw + 1 < len(waves_sc):
                    emit_scores(*waves_sc[iw + 1])
                emit_pt(tch_, w_)
                emit_av(tch_, w_)
        NW_ = 8 * TCN
        eng.wait_ge(s1h, NW_)
        eng.wait_ge(v1l, NW_)
        eng.wait_ge(s1o, NW_)
        eng.wait_ge(v1o, NW_)
        eng.transpose(p1q[0][:, 0:128], c.idf_t[:], c.idf_t[:]).then_inc(c.phz, 1)

    es.close()


def _phase15(c):
    nc, blk, T, TCN = c.nc, c.blk, c.T, c.TCN
    es = ExitStack()
    sb = lambda n, s, d: es.enter_context(nc.sbuf_tensor(n, s, d))
    psm = lambda n, s, d=F32: es.enter_context(nc.psum_tensor(n, s, d))
    PB = 4 * c.PH

    woh_s = sb("woh_s", [128, 16 * 2048], BF16)
    wol_s = sb("wol_s", [128, 16 * 2048], BF16)
    osb_h = [sb("osb_h0", [128, 2048], BF16), sb("osb_h1", [128, 2048], BF16)]
    osb_l = [sb("osb_l0", [128, 2048], BF16), sb("osb_l1", [128, 2048], BF16)]
    xb15 = sb("xb15", [128, 2048], F32)
    ty = [sb("ty0", [128, 2048], F32), sb("ty1", [128, 2048], F32)]
    bo_sb = sb("bo_sb", [1, 2048], F32)
    bo_b = sb("bo_b", [128, 2048], F32)
    wsb = [sb("wsb0", [128, 2048], F32), sb("wsb1", [128, 2048], F32)]
    p15 = [psm("p15_0", [128, 2048], F32), psm("p15_1", [128, 2048], F32)]

    q15w = c.DQ("q15w", 1)
    q15b = c.DQ("q15b", 1)
    q15o = c.DQ("q15o", 2)
    q15x = c.DQ("q15x", 1)
    q15y = c.DQ("q15y", 2)
    t15 = c.sem("t15"); v15 = c.sem("v15"); t15b = c.sem("t15b"); v15b = c.sem("v15b")
    qZ = c.DQ("qZ", 1)
    qY = c.DQ("qY", 1)
    v1z = c.sem("v1z"); s1z = c.sem("s1z")
    mk = {}
    SJ = []
    for wi_, wd_ in enumerate([c.wgT_d, c.wuT_d]):
        for r0 in range(0, E, 128):
            for cq in range(4):
                SJ.append((wd_, r0, cq * 2048, 2048, 64 * wi_ + (r0 // 128) * 4 + cq))
    for r0 in range(0, II, 128):
        SJ.append((c.wdT_d, r0, 0, 2048, 128 + r0 // 128))
    NSJ = len(SJ)
    SJV = [j for j in range(NSJ) if j % 2 == 0]
    SJS = [j for j in range(NSJ) if j % 2 == 1]

    @blk.sync
    def _(eng):
        eng.wait_ge(c.phz, PB)
        mk["bo"] = q15b.inc(eng.dma_start(out=bo_sb[:], in_=c.bo_d), 0)
        for hdc in range(16):
            q15w.inc(eng.dma_start(out=woh_s[:, hdc * 2048:(hdc + 1) * 2048],
                                   in_=c.woh_d[hdc * 128:(hdc + 1) * 128, :]), 0)
            mk["wo"] = q15w.inc(eng.dma_start(out=wol_s[:, hdc * 2048:(hdc + 1) * 2048],
                                              in_=c.wol_d[hdc * 128:(hdc + 1) * 128, :]), 0)
        for tch in range(TCN):
            if tch >= 2:
                eng.wait_ge(t15, tch - 1)
            ob = tch % 2
            oh3 = osb_h[ob][:].rearrange("p (hc t) -> p hc t", hc=16)
            ol3 = osb_l[ob][:].rearrange("p (hc t) -> p hc t", hc=16)
            q15o.inc(eng.dma_start(
                out=oh3, in_=c.oh_d.transpose([1, 0, 2])[:, :, tch * 128:(tch + 1) * 128]), ob)
            mk[("o", tch)] = q15o.inc(eng.dma_start(
                out=ol3, in_=c.ol_d.transpose([1, 0, 2])[:, :, tch * 128:(tch + 1) * 128]), ob)
            if tch >= 1:
                eng.wait_ge(v15, tch)
            mk[("x", tch)] = q15x.inc(eng.dma_start(
                out=xb15[:], in_=c.x_d[tch * 128:(tch + 1) * 128, :]), 0)
            if tch >= 1:
                tp = tch - 1
                eng.wait_ge(v15, tp + 1)
                mk[("ysp", tp)] = q15y.inc(eng.dma_start(
                    out=c.ysp_d[tp * 128:(tp + 1) * 128, :], in_=ty[tp % 2][:]), tp % 2)
            nsjs = len(SJS)
            for k in range(nsjs * tch // TCN, nsjs * (tch + 1) // TCN):
                j = SJS[k]
                (wd_, r0, c0, w, col) = SJ[j]
                if k >= 1:
                    eng.wait_ge(s1z, k)
                mk[("z", j)] = qY.inc(eng.dma_start(
                    out=wsb[1][:, 0:w], in_=wd_[r0:r0 + 128, c0:c0 + w]), 0)
        tp = TCN - 1
        eng.wait_ge(v15, tp + 1)
        mk[("ysp", tp)] = q15y.inc(eng.dma_start(
            out=c.ysp_d[tp * 128:(tp + 1) * 128, :], in_=ty[tp % 2][:]), tp % 2)

    @blk.gpsimd
    def _(eng):
        eng.wait_ge(c.phz, PB)
        for k, j in enumerate(SJV):
            (wd_, r0, c0, w, col) = SJ[j]
            if k >= 1:
                eng.wait_ge(v1z, k)
            mk[("z", j)] = qZ.inc(eng.dma_start(out=wsb[0][:, 0:w],
                                                in_=wd_[r0:r0 + 128, c0:c0 + w]),
                                  0)
        eng.memset(c.dum[:, 2:3], 0.0).then_inc(c.phz, 1)


    @blk.tensor
    def _(eng):
        eng.wait_ge(c.phz, PB)
        q15b.wait(eng, mk["bo"])
        for eb in range(4):
            eng.matmul(p15[0][:, eb * 512:(eb + 1) * 512], c.on1_t[:],
                       bo_sb[0:1, eb * 512:(eb + 1) * 512], start=True,
                       stop=True).then_inc(t15b, 1)
        q15w.wait(eng, mk["wo"])
        for tch in range(TCN):
            pp = p15[tch % 2]
            q15o.wait(eng, mk[("o", tch)])
            if tch == 0:
                eng.wait_ge(v15b, 1)
            if tch >= 2:
                eng.wait_ge(v15, tch - 1)
            for hdc in range(16):
                lh = osb_h[tch % 2][:, hdc * 128:(hdc + 1) * 128]
                ll = osb_l[tch % 2][:, hdc * 128:(hdc + 1) * 128]
                for eb in range(4):
                    a = pp[:, eb * 512:(eb + 1) * 512]
                    wh = woh_s[:, hdc * 2048 + eb * 512:hdc * 2048 + (eb + 1) * 512]
                    wl = wol_s[:, hdc * 2048 + eb * 512:hdc * 2048 + (eb + 1) * 512]
                    eng.matmul(a, lh, wh, start=(hdc == 0), stop=False)
                    eng.matmul(a, lh, wl, start=False, stop=False)
                    ii = eng.matmul(a, ll, wh, start=False, stop=(hdc == 15))
                    if hdc == 15 and eb == 3:
                        ii.then_inc(t15, 1)
        eng.wait_ge(v15, TCN)
        eng.transpose(p15[0][:, 0:128], c.idf_t[:], c.idf_t[:]).then_inc(c.phz, 1)

    @blk.vector
    def _(eng):
        eng.wait_ge(c.phz, PB)
        eng.wait_ge(t15b, 4)
        eng.tensor_copy(out=bo_b[:], in_=p15[0][:]).then_inc(v15b, 1)
        for tch in range(TCN):
            eng.wait_ge(t15, tch + 1)
            q15x.wait(eng, mk[("x", tch)])
            if tch >= 2:
                q15y.wait(eng, mk[("ysp", tch - 2)])
            eng.tensor_add(out=ty[tch % 2][:], in0=p15[tch % 2][:], in1=bo_b[:])
            eng.drain()
            eng.tensor_add(out=ty[tch % 2][:], in0=ty[tch % 2][:],
                           in1=xb15[:]).then_inc(v15, 1)
            nsjv = len(SJV)
            for k in range(nsjv * tch // TCN, nsjv * (tch + 1) // TCN):
                j = SJV[k]
                (wd_, r0, c0, w, col) = SJ[j]
                qZ.wait(eng, mk[("z", j)])
                eng.tensor_reduce(out=c.wsum[:, col:col + 1], in_=wsb[0][:, 0:w],
                                  axis=AX, op=OP.add,
                                  apply_absolute_value=True).then_inc(v1z, 1)
        q15y.wait(eng, mk[("ysp", TCN - 2)])
        q15y.wait(eng, mk[("ysp", TCN - 1)])
        eng.memset(c.dum[:, 1:2], 0.0).then_inc(c.phz, 1)

    @blk.scalar
    def _(eng):
        eng.wait_ge(c.phz, PB)
        for j in SJS:
            (wd_, r0, c0, w, col) = SJ[j]
            qY.wait(eng, mk[("z", j)])
            eng.activation(out=wsb[1][:, 0:w], in_=wsb[1][:, 0:w],
                           func=AF.Abs, bias=0.0, scale=1.0,
                           accum_out=c.wsum[:, col:col + 1]).then_inc(s1z, 1)
        eng.activation(out=c.dum[:, 0:1], in_=c.u_t[:], func=AF.Copy, bias=0.0,
                       scale=0.0).then_inc(c.phz, 1)

    es.close()


def _phase2(c):
    nc, blk, T, TCN = c.nc, c.blk, c.T, c.TCN
    es = ExitStack()
    sb = lambda n, s, d: es.enter_context(nc.sbuf_tensor(n, s, d))
    psm = lambda n, s, d=F32: es.enter_context(nc.psum_tensor(n, s, d))
    PB = 4 * c.PH
    IBN = 16

    xqf = sb("xqf", [128, 16 * T], BF16)
    wgp = [sb("wgp0", [128, 512], F32), sb("wgp1", [128, 512], F32)]
    wup = [sb("wup0", [128, 512], F32), sb("wup1", [128, 512], F32)]
    sg1 = [sb("sg1_0", [128, 512], F32), sb("sg1_1", [128, 512], F32)]
    sg2 = [sb("sg2_0", [128, 512], F32), sb("sg2_1", [128, 512], F32)]
    su1 = [sb("su1_0", [128, 512], F32), sb("su1_1", [128, 512], F32)]
    su2 = [sb("su2_0", [128, 512], F32), sb("su2_1", [128, 512], F32)]
    wgq = [sb("wgq0", [128, 16 * 512], BF16), sb("wgq1", [128, 16 * 512], BF16)]
    wuq = [sb("wuq0", [128, 16 * 512], BF16), sb("wuq1", [128, 16 * 512], BF16)]
    gact = [sb("gact0", [128, 512], F32), sb("gact1", [128, 512], F32)]
    zt = [sb("zt0", [128, 512], F32), sb("zt1", [128, 512], F32)]
    sm2 = sb("sm2", [128, 4], F32)
    ws3 = sb("ws3", [128, 4], F32)
    sb3 = sb("sb3", [128, 4], F32)
    row3 = sb("row3", [1, 4], F32)
    p2 = [psm("p2_0", [128, 512], F32), psm("p2_1", [128, 512], F32),
          psm("p2_2", [128, 512], F32), psm("p2_3", [128, 512], F32)]

    q2x = c.DQ("q2x", 1)
    q2w = c.DQ("q2w", 2)
    q2z = c.DQ("q2z", 2)
    t2 = c.sem("t2"); s2g = c.sem("s2g"); v2z = c.sem("v2z")
    s2q = c.sem("s2q"); v2q = c.sem("v2q"); g2q = c.sem("g2q")
    t2s = c.sem("t2s"); v2s = c.sem("v2s")
    mk = {}

    @blk.sync
    def _(eng):
        eng.wait_ge(c.phz, PB)
        for ec in range(16):
            mk["xq"] = q2x.inc(eng.dma_start(out=xqf[:, ec * T:(ec + 1) * T],
                                             in_=c.xq_d[ec, :, :]), 0)
        for ib in range(IBN):
            for tch in range(TCN):
                idx = ib * TCN + tch
                eng.wait_ge(v2z, idx + 1)
                mk[("z", idx)] = q2z.inc(eng.dma_start(
                    out=c.z_d[tch * 128:(tch + 1) * 128, ib * 512:(ib + 1) * 512],
                    in_=zt[idx % 2][:]), idx % 2)

    @blk.scalar
    def _(eng):
        eng.wait_ge(c.phz, PB)
        for ib in range(IBN + 1):
            for j in range(16):
                if ib < IBN:
                    idx = ib * 16 + j
                    if idx >= 2:
                        eng.wait_ge(s2q, idx - 1)
                        eng.wait_ge(v2q, idx - 1)
                    q2w.inc(eng.dma_start(
                        out=wgp[idx % 2][:, 0:512],
                        in_=c.wgT_d[j * 128:(j + 1) * 128, ib * 512:(ib + 1) * 512]),
                        idx % 2)
                    mk[("w", idx)] = q2w.inc(eng.dma_start(
                        out=wup[idx % 2][:, 0:512],
                        in_=c.wuT_d[j * 128:(j + 1) * 128, ib * 512:(ib + 1) * 512]),
                        idx % 2)
                    if idx == 0:
                        eng.wait_ge(v2s, 4)
                    q2w.wait(eng, mk[("w", idx)])
                    if idx >= 2:
                        eng.wait_ge(g2q, idx - 1)
                    eng.activation(out=sg1[idx % 2][:], in_=wgp[idx % 2][:, 0:512],
                                   func=AF.Sign, bias=c.wsc[:, 1:2], scale=1.0)
                    eng.activation(out=sg2[idx % 2][:], in_=wgp[idx % 2][:, 0:512],
                                   func=AF.Sign, bias=c.wsc[:, 0:1],
                                   scale=1.0).then_inc(s2q, 1)
                if ib >= 1 and j < TCN:
                    ibp, tch = ib - 1, j
                    idz = ibp * TCN + tch
                    eng.wait_ge(t2, idz + 1)
                    if idz >= 2:
                        eng.wait_ge(v2z, idz - 1)
                    eng.activation(out=gact[idz % 2][:], in_=p2[2 * (idz % 2)][:],
                                   func=AF.Silu, bias=0.0,
                                   scale=c.xswg[:, tch:tch + 1]).then_inc(s2g, 1)
        eng.activation(out=c.dum[:, 0:1], in_=c.u_t[:], func=AF.Copy, bias=0.0,
                       scale=0.0).then_inc(c.phz, 1)

    @blk.gpsimd
    def _(eng):
        eng.wait_ge(c.phz, PB)
        for ib in range(IBN):
            if ib >= 2:
                eng.wait_ge(t2, (ib - 1) * TCN)
            for j in range(16):
                idx = ib * 16 + j
                eng.wait_ge(s2q, idx + 1)
                eng.wait_ge(v2q, idx + 1)
                eng.tensor_add(out=wgq[ib % 2][:, j * 512:(j + 1) * 512],
                               in0=sg1[idx % 2][:], in1=sg2[idx % 2][:])
                eng.drain()
                eng.tensor_sub(out=wuq[ib % 2][:, j * 512:(j + 1) * 512],
                               in0=su1[idx % 2][:], in1=su2[idx % 2][:])
                eng.drain()
                eng.tensor_copy(out=c.dum[:, 3:4], in_=c.u_t[:]).then_inc(g2q, 1)
        eng.memset(c.dum[:, 2:3], 0.0).then_inc(c.phz, 1)

    @blk.vector
    def _(eng):
        eng.wait_ge(c.phz, PB)
        eng.tensor_reduce(out=ws3[:, 0:1], in_=c.wsum[:, 0:64], axis=AX, op=OP.add)
        eng.tensor_reduce(out=ws3[:, 1:2], in_=c.wsum[:, 64:128], axis=AX, op=OP.add)
        eng.tensor_reduce(out=ws3[:, 2:3], in_=c.wsum[:, 128:192], axis=AX, op=OP.add)
        eng.memset(ws3[:, 3:4], 1.0)
        eng.drain()
        eng.tensor_copy(out=c.dum[:, 4:5], in_=ws3[:, 0:1]).then_inc(v2s, 1)
        eng.wait_ge(t2s, 1)
        eng.tensor_copy(out=sb3[0:3, 0:1], in_=p2[0][0:3, 0:1]).then_inc(v2s, 1)
        eng.wait_ge(t2s, 2)
        eng.tensor_copy(out=row3[0:1, 0:3], in_=p2[1][0:1, 0:3]).then_inc(v2s, 1)
        eng.wait_ge(t2s, 3)
        eng.tensor_scalar(out=c.wsc[:, 0:1], in0=p2[2][:, 0:1],
                          scalar1=float(2.0 ** -25), scalar2=5e-6,
                          op0=OP.mult, op1=OP.add)
        eng.tensor_scalar(out=c.wsc[:, 2:3], in0=p2[2][:, 1:2],
                          scalar1=float(2.0 ** -25), scalar2=5e-6,
                          op0=OP.mult, op1=OP.add)
        eng.tensor_scalar(out=c.wsc[:, 4:5], in0=p2[2][:, 2:3],
                          scalar1=float(2.0 ** -25), scalar2=5e-6,
                          op0=OP.mult, op1=OP.add)
        eng.drain()
        eng.tensor_scalar(out=c.wsc[:, 1:2], in0=c.wsc[:, 0:1], scalar1=-1.0,
                          scalar2=None, op0=OP.mult)
        eng.tensor_scalar(out=c.wsc[:, 3:4], in0=c.wsc[:, 2:3], scalar1=-1.0,
                          scalar2=None, op0=OP.mult)
        eng.tensor_scalar(out=c.wsc[:, 5:6], in0=c.wsc[:, 4:5], scalar1=-1.0,
                          scalar2=None, op0=OP.mult)
        eng.drain()
        eng.tensor_scalar(out=c.xswg[:], in0=c.xs_all[:], scalar1=c.wsc[:, 0:1],
                          scalar2=None, op0=OP.mult)
        eng.tensor_scalar(out=c.xswu[:], in0=c.xs_all[:], scalar1=c.wsc[:, 2:3],
                          scalar2=2.0, op0=OP.mult, op1=OP.mult)
        eng.drain()
        eng.tensor_copy(out=c.dum[:, 5:6], in_=c.xswg[:, 0:1]).then_inc(v2s, 1)
        for ib in range(IBN + 1):
            for j in range(16):
                if ib < IBN:
                    idx = ib * 16 + j
                    q2w.wait(eng, mk[("w", idx)])
                    if idx >= 2:
                        eng.wait_ge(g2q, idx - 1)
                    eng.tensor_scalar(out=su1[idx % 2][:], in0=wup[idx % 2][:, 0:512],
                                      scalar1=c.wsc[:, 2:3], scalar2=None,
                                      op0=OP.is_ge)
                    eng.tensor_scalar(out=su2[idx % 2][:], in0=wup[idx % 2][:, 0:512],
                                      scalar1=c.wsc[:, 3:4], scalar2=None,
                                      op0=OP.is_le).then_inc(v2q, 1)
                if ib >= 1 and j < TCN:
                    ibp, tch = ib - 1, j
                    idz = ibp * TCN + tch
                    eng.wait_ge(s2g, idz + 1)
                    if idz >= 2:
                        q2z.wait(eng, mk[("z", idz - 2)])
                    eng.scalar_tensor_tensor(out=zt[idz % 2][:], in0=gact[idz % 2][:],
                                             scalar=c.xswu[:, tch:tch + 1],
                                             in1=p2[2 * (idz % 2) + 1][:],
                                             op0=OP.mult, op1=OP.mult)
                    eng.drain()
                    eng.tensor_reduce(out=sm2[:, 0:1], in_=zt[idz % 2][:], axis=AX,
                                      op=OP.max, apply_absolute_value=True)
                    eng.drain()
                    eng.tensor_tensor(out=c.rmax_z[:, tch:tch + 1],
                                      in0=c.rmax_z[:, tch:tch + 1], in1=sm2[:, 0:1],
                                      op=OP.max).then_inc(v2z, 1)
        eng.drain()
        eng.tensor_scalar(out=c.s2i_all[:], in0=c.rmax_z[:], scalar1=c.r127[:],
                          scalar2=1e-5, op0=OP.mult, op1=OP.add)
        eng.drain()
        eng.tensor_scalar(out=c.s2w_all[:], in0=c.s2i_all[:],
                          scalar1=c.wsc[:, 4:5], scalar2=None, op0=OP.mult)
        eng.drain()
        eng.reciprocal(out=c.s2i_all[:], in_=c.s2i_all[:])
        q2z.wait(eng, mk[("z", IBN * TCN - 2)])
        q2z.wait(eng, mk[("z", IBN * TCN - 1)])
        q2x.wait(eng, mk["xq"])
        eng.memset(c.dum[:, 1:2], 0.0).then_inc(c.phz, 1)

    @blk.tensor
    def _(eng):
        eng.wait_ge(c.phz, PB)
        eng.wait_ge(v2s, 1)
        eng.matmul(p2[0][0:3, 0:1], ws3[:, 0:3], ws3[:, 3:4], start=True,
                   stop=True).then_inc(t2s, 1)
        eng.wait_ge(v2s, 2)
        eng.transpose(p2[1][0:1, 0:3], sb3[0:3, 0:1],
                      c.idf_t[0:3, 0:3]).then_inc(t2s, 1)
        eng.wait_ge(v2s, 3)
        eng.matmul(p2[2][:, 0:3], c.on1_t[:], row3[0:1, 0:3], start=True,
                   stop=True).then_inc(t2s, 1)
        q2x.wait(eng, mk["xq"])
        for ib in range(IBN):
            eng.wait_ge(g2q, 16 * (ib + 1))
            for tch in range(TCN):
                idx = ib * TCN + tch
                if idx >= 2:
                    eng.wait_ge(s2g, idx - 1)
                    eng.wait_ge(v2z, idx - 1)
                elif ib == 0 and tch == 0:
                    eng.wait_ge(v2s, 3)
                pg = p2[2 * (idx % 2)]
                pu = p2[2 * (idx % 2) + 1]
                for ec in range(16):
                    lt = xqf[:, ec * T + tch * 128:ec * T + (tch + 1) * 128]
                    eng.matmul(pg[:], lt, wgq[ib % 2][:, ec * 512:(ec + 1) * 512],
                               start=(ec == 0), stop=(ec == 15))
                    ii = eng.matmul(pu[:], lt, wuq[ib % 2][:, ec * 512:(ec + 1) * 512],
                                    start=(ec == 0), stop=(ec == 15))
                    if ec == 15:
                        ii.then_inc(t2, 1)
        eng.wait_ge(s2g, IBN * TCN)
        eng.wait_ge(v2z, IBN * TCN)
        eng.transpose(p2[0][:, 0:128], c.idf_t[:], c.idf_t[:]).then_inc(c.phz, 1)

    es.close()


def _phase3(c):
    nc, blk, T, TCN = c.nc, c.blk, c.T, c.TCN
    es = ExitStack()
    sb = lambda n, s, d: es.enter_context(nc.sbuf_tensor(n, s, d))
    psm = lambda n, s, d=F32: es.enter_context(nc.psum_tensor(n, s, d))
    PB = 4 * c.PH
    NQ = 4

    wdq = sb("wdq", [128, 64 * 512], BF16)
    wdp = [sb("wdp0", [128, 512], F32), sb("wdp1", [128, 512], F32)]
    sd1 = [sb("sd1_0", [128, 512], F32), sb("sd1_1", [128, 512], F32)]
    sd2 = [sb("sd2_0", [128, 512], F32), sb("sd2_1", [128, 512], F32)]
    zp = [sb("zp0", [128, 2048], F32), sb("zp1", [128, 2048], F32)]
    vv3 = [sb("vv3a", [128, 2048], F32), sb("vv3b", [128, 2048], F32)]
    zqp = [sb("zqp0", [128, 2048], BF16), sb("zqp1", [128, 2048], BF16)]
    zqf = [sb("zqf0", [128, 64 * 128], BF16), sb("zqf1", [128, 64 * 128], BF16)]
    t3sb = [sb("t3sb0", [128, 512], F32), sb("t3sb1", [128, 512], F32)]
    outb = [sb("outb0", [128, 512], F32), sb("outb1", [128, 512], F32)]
    ybuf = [sb("ybuf0", [128, 512], F32), sb("ybuf1", [128, 512], F32)]
    p3 = [psm("p3_0", [128, 512], F32), psm("p3_1", [128, 512], F32)]
    p3t = [psm("p3t0", [128, 1024], BF16), psm("p3t1", [128, 1024], BF16)]

    q3w = c.DQ("q3w", 2)
    q3z = c.DQ("q3z", 2)
    q3q = c.DQ("q3q", 2)
    q3s = c.DQ("q3s", 2)
    q3y = c.DQ("q3y", 2)
    qO = c.DQ("qO", 2)
    t3 = c.sem("t3"); t3f = c.sem("t3f"); t3t = c.sem("t3t")
    s3v = c.sem("s3v"); g3 = c.sem("g3"); v3q = c.sem("v3q")
    v3e = c.sem("v3e")
    s3q = c.sem("s3q"); g3w = c.sem("g3w")
    s3e = c.sem("s3e"); v3y = c.sem("v3y")
    mk = {}

    @blk.sync
    def _(eng):
        eng.wait_ge(c.phz, PB)
        for q in range(NQ):
            for tch in range(TCN):
                idx = q * TCN + tch
                if q == 0:
                    for p in range(4):
                        pidx = tch * 4 + p
                        if pidx >= 2:
                            eng.wait_ge(s3v, pidx - 1)
                        mk[("zl", pidx)] = q3z.inc(eng.dma_start(
                            out=zp[pidx % 2][:],
                            in_=c.z_d[tch * 128:(tch + 1) * 128,
                                      p * 2048:(p + 1) * 2048]), pidx % 2)
                    if tch >= 1:
                        eng.wait_ge(v3e, 8 * tch)
                        zq3 = zqf[(tch - 1) % 2][:].rearrange(
                            "p (ic t) -> p ic t", ic=64)
                        mk[("zs", tch - 1)] = q3s.inc(eng.dma_start(
                            out=c.zq_d.transpose([1, 0, 2])[
                                :, :, (tch - 1) * 128:tch * 128],
                            in_=zq3), (tch - 1) % 2)
                else:
                    if q == 1 and tch == 0:
                        tchl = TCN - 1
                        eng.wait_ge(v3e, 8 * TCN)
                        zq3l = zqf[tchl % 2][:].rearrange("p (ic t) -> p ic t", ic=64)
                        mk[("zs", tchl)] = q3s.inc(eng.dma_start(
                            out=c.zq_d.transpose([1, 0, 2])[
                                :, :, tchl * 128:(tchl + 1) * 128],
                            in_=zq3l), tchl % 2)
                        q3s.wait(eng, mk[("zs", TCN - 2)])
                        q3s.wait(eng, mk[("zs", TCN - 1)])
                    if True:
                        if tch >= 2 or q >= 2:
                            eng.wait_ge(t3, idx - 1)
                        mk[("zq", idx)] = q3q.inc(eng.dma_start(
                            out=zqf[tch % 2][:].rearrange("p (ic t) -> p ic t",
                                                          ic=64),
                            in_=c.zq_d.transpose([1, 0, 2])[
                                :, :, tch * 128:(tch + 1) * 128]), tch % 2)
                if idx >= 2:
                    eng.wait_ge(v3y, idx - 1)
                mk[("y", idx)] = q3y.inc(eng.dma_start(
                    out=ybuf[idx % 2][:],
                    in_=c.ysp_d[tch * 128:(tch + 1) * 128,
                                q * 512:(q + 1) * 512]), idx % 2)
                if idx >= 1:
                    pi = idx - 1
                    qq, tt = pi // TCN, pi % TCN
                    eng.wait_ge(v3y, pi + 1)
                    mk[("o", pi)] = qO.inc(eng.dma_start(
                        out=c.y_d[tt * 128:(tt + 1) * 128, qq * 512:(qq + 1) * 512],
                        in_=outb[pi % 2][:]), pi % 2)
        pi = NQ * TCN - 1
        qq, tt = pi // TCN, pi % TCN
        eng.wait_ge(v3y, pi + 1)
        mk[("o", pi)] = qO.inc(eng.dma_start(
            out=c.y_d[tt * 128:(tt + 1) * 128, qq * 512:(qq + 1) * 512],
            in_=outb[pi % 2][:]), pi % 2)
        qO.wait(eng, mk[("o", NQ * TCN - 2)])
        qO.wait(eng, mk[("o", NQ * TCN - 1)])

    @blk.scalar
    def _(eng):
        eng.wait_ge(c.phz, PB)
        for q in range(NQ):
            for ic in range(64):
                jdx = q * 64 + ic
                if jdx >= 2:
                    eng.wait_ge(s3q, jdx - 1)
                mk[("wd", jdx)] = q3w.inc(eng.dma_start(
                    out=wdp[jdx % 2][:],
                    in_=c.wdT_d[ic * 128:(ic + 1) * 128, q * 512:(q + 1) * 512]),
                    jdx % 2)
                q3w.wait(eng, mk[("wd", jdx)])
                if jdx >= 2:
                    eng.wait_ge(g3w, jdx - 1)
                eng.activation(out=sd1[jdx % 2][:], in_=wdp[jdx % 2][:],
                               func=AF.Sign, bias=c.wsc[:, 5:6], scale=1.0)
                eng.activation(out=sd2[jdx % 2][:], in_=wdp[jdx % 2][:],
                               func=AF.Sign, bias=c.wsc[:, 4:5],
                               scale=1.0).then_inc(s3q, 1)
            for tch in range(TCN):
                idx = q * TCN + tch
                if q == 0:
                    for p in range(4):
                        pidx = tch * 4 + p
                        q3z.wait(eng, mk[("zl", pidx)])
                        if pidx >= 2:
                            eng.wait_ge(v3q, pidx - 1)
                        eng.activation(out=vv3[pidx % 2][:], in_=zp[pidx % 2][:],
                                       func=AF.Copy, bias=0.0,
                                       scale=c.s2i_all[:, tch:tch + 1]
                                       ).then_inc(s3v, 1)
                eng.wait_ge(t3, idx + 1)
                if idx >= 2:
                    eng.wait_ge(v3y, idx - 1)
                eng.activation(out=t3sb[idx % 2][:], in_=p3[idx % 2][:],
                               func=AF.Copy, bias=0.0,
                               scale=c.s2w_all[:, tch:tch + 1]).then_inc(s3e, 1)
        eng.activation(out=c.dum[:, 0:1], in_=c.u_t[:], func=AF.Copy, bias=0.0,
                       scale=0.0).then_inc(c.phz, 1)

    @blk.gpsimd
    def _(eng):
        eng.wait_ge(c.phz, PB)
        for q in range(NQ):
            for ic in range(64):
                jdx = q * 64 + ic
                eng.wait_ge(s3q, jdx + 1)
                if q >= 1:
                    icg = ic // 8
                    eng.wait_ge(t3f, 8 * (q - 1) + icg + 1)
                eng.tensor_add(out=wdq[:, ic * 512:(ic + 1) * 512],
                               in0=sd1[jdx % 2][:], in1=sd2[jdx % 2][:])
                eng.drain()
                eng.tensor_copy(out=c.dum[:, 3:4], in_=c.u_t[:]).then_inc(g3w, 1)
        eng.memset(c.dum[:, 2:3], 0.0).then_inc(c.phz, 1)

    @blk.vector
    def _(eng):
        eng.wait_ge(c.phz, PB)
        for q in range(NQ):
            for tch in range(TCN):
                idx = q * TCN + tch
                if q == 0:
                    if tch >= 2:
                        eng.wait_ge(t3, idx - 1)
                        q3s.wait(eng, mk[("zs", tch - 2)])
                    for p in range(4):
                        pidx = tch * 4 + p
                        eng.wait_ge(s3v, pidx + 1)
                        if pidx >= 2:
                            eng.wait_ge(t3t, 2 * (pidx - 1))
                        eng.tensor_scalar(out=zqp[pidx % 2][:], in0=vv3[pidx % 2][:],
                                          scalar1=c.cr_t[:], scalar2=c.cr_t[:],
                                          op0=OP.add,
                                          op1=OP.subtract).then_inc(v3q, 1)
                        eng.drain()
                        if p >= 1:
                            for g in (2 * (p - 1), 2 * (p - 1) + 1):
                                eng.wait_ge(t3t, 8 * tch + g + 1)
                                eng.tensor_copy(
                                    out=zqf[tch % 2][:, g * 1024:(g + 1) * 1024],
                                    in_=p3t[g % 2][:]).then_inc(v3e, 1)
                    for g in (6, 7):
                        eng.wait_ge(t3t, 8 * tch + g + 1)
                        eng.tensor_copy(
                            out=zqf[tch % 2][:, g * 1024:(g + 1) * 1024],
                            in_=p3t[g % 2][:]).then_inc(v3e, 1)
                eng.wait_ge(s3e, idx + 1)
                q3y.wait(eng, mk[("y", idx)])
                if idx >= 2:
                    qO.wait(eng, mk[("o", idx - 2)])
                eng.tensor_add(out=outb[idx % 2][:], in0=t3sb[idx % 2][:],
                               in1=ybuf[idx % 2][:]).then_inc(v3y, 1)
        eng.memset(c.dum[:, 1:2], 0.0).then_inc(c.phz, 1)

    @blk.tensor
    def _(eng):
        eng.wait_ge(c.phz, PB)
        for q in range(NQ):
            for tch in range(TCN):
                idx = q * TCN + tch
                if q == 0:
                    for g in range(8):
                        pidx = tch * 4 + g // 2
                        eng.wait_ge(v3q, pidx + 1)
                        gidx = 8 * tch + g
                        if gidx >= 2:
                            eng.wait_ge(v3e, gidx - 1)
                        for k in range(8):
                            col = (g % 2) * 1024 + k * 128
                            src = zqp[pidx % 2][:, (g % 2) * 1024 + k * 128:
                                                (g % 2) * 1024 + (k + 1) * 128]
                            ii = eng.transpose(p3t[g % 2][:, k * 128:(k + 1) * 128],
                                               src, c.idb_t[:])
                            if k == 7:
                                ii.then_inc(t3t, 1)
                    eng.wait_ge(v3e, 8 * (tch + 1))
                else:
                    q3q.wait(eng, mk[("zq", idx)])
                if idx >= 2:
                    eng.wait_ge(s3e, idx - 1)
                for icg in range(8):
                    eng.wait_ge(g3w, 64 * q + 8 * (icg + 1))
                    for ic8 in range(8):
                        ic = icg * 8 + ic8
                        lt = zqf[tch % 2][:, ic * 128:(ic + 1) * 128]
                        ii = eng.matmul(p3[idx % 2][:], lt,
                                        wdq[:, ic * 512:(ic + 1) * 512],
                                        start=(ic == 0), stop=(ic == 63))
                        if ic == 63:
                            ii.then_inc(t3, 1)
                        if tch == TCN - 1 and ((ic8 == 7 and ic != 63) or ic == 62):
                            ii.then_inc(t3f, 1)
        eng.wait_ge(s3e, NQ * TCN)
        eng.transpose(p3t[0][:, 0:128], c.idb_t[:], c.idb_t[:]).then_inc(c.phz, 1)

    es.close()


_BUILD_CACHE = {}


def _get_nc(T):
    if T not in _BUILD_CACHE:
        _BUILD_CACHE[T] = build_nc(T)
    return _BUILD_CACHE[T]


def _hd_perm():
    perm = np.zeros(HD, np.int64)
    for b in range(16):
        for r in range(128):
            head = 8 * (b // 4) + (b % 4) + (4 if r >= 64 else 0)
            perm[b * 128 + r] = head * 64 + (r % 64)
    return perm


def _prep_inputs(T, L0_attn_x, L0_attn_Wq, L0_attn_bq, L0_attn_Wk, L0_attn_bk,
                 L0_attn_Wv, L0_attn_bv, L0_attn_Wo, L0_attn_bo, L0_attn_sinks,
                 L0_ffn_w_gate, L0_ffn_w_up, L0_ffn_w_down):
    cst = _host_consts(T)
    perm = _hd_perm()
    shared = dict(
        Wq=np.ascontiguousarray(np.asarray(L0_attn_Wq, np.float32)[:, perm]),
        Wk=np.ascontiguousarray(L0_attn_Wk, np.float32),
        Wv=np.ascontiguousarray(L0_attn_Wv, np.float32),
        Wo=np.ascontiguousarray(np.asarray(L0_attn_Wo, np.float32)[perm, :]),
        bq_r=np.ascontiguousarray(
            np.asarray(L0_attn_bq, np.float32)[perm].reshape(HDCN, 128).T),
        bk_r=np.ascontiguousarray(np.asarray(L0_attn_bk, np.float32).reshape(4, 128).T),
        bv_r=np.ascontiguousarray(np.asarray(L0_attn_bv, np.float32).reshape(4, 128).T),
        bo_row=np.ascontiguousarray(np.asarray(L0_attn_bo, np.float32).reshape(1, E)),
        sinks_b=np.ascontiguousarray(
            np.broadcast_to(np.asarray(L0_attn_sinks, np.float32)[None, :], (128, H))),
        wgT=np.ascontiguousarray(np.asarray(L0_ffn_w_gate, np.float32).T),
        wuT=np.ascontiguousarray(np.asarray(L0_ffn_w_up, np.float32).T),
        wdT=np.ascontiguousarray(np.asarray(L0_ffn_w_down, np.float32).T),
        cos_fm=cst["cos_fm"], sin_fm=cst["sin_fm"], idb=cst["idb"], idf=cst["idf"],
        RT=cst["RT"], ones1=cst["ones1"], mask0=cst["mask0"], mask1=cst["mask1"],
        mask2=cst["mask2"],
    )
    xs = np.ascontiguousarray(np.asarray(L0_attn_x, np.float32))
    in_maps = []
    for b in range(xs.shape[0]):
        m = dict(shared)
        m["x"] = np.ascontiguousarray(xs[b, :T, :])
        in_maps.append(m)
    return in_maps


def kernel(**inputs):
    T = 2048
    nc = _get_nc(T)
    in_maps = _prep_inputs(T, **inputs)
    res = run_bass_kernel_spmd(nc, in_maps, core_ids=list(range(len(in_maps))))
    return np.stack([r["y"] for r in res.results], axis=0)
